# version 05
# speedup vs baseline: 1.1928x; 1.0805x over previous
import numpy as np
import ml_dtypes
from contextlib import ExitStack

import concourse.bass as bass
import concourse.mybir as mybir
from concourse.bass_utils import run_bass_kernel_spmd

F32 = mybir.dt.float32
BF16 = mybir.dt.bfloat16
AF = mybir.ActivationFunctionType
ALU = mybir.AluOpType
AXX = mybir.AxisListType.X
BF = ml_dtypes.bfloat16

NCORES = 8
T = 1024
D = 4096
FT = D // 128
NTC = 2
TC = 512
EPS = 1e-5
L1_SHIFT = -16.0
L1_SCALE = float(1.0 / np.sqrt(128.0))
NQ = 3484.0
NV = 1228.0
NTX = 4096.0 - NV


def _c(a):
    return np.ascontiguousarray(a)


def _blocked(w):
    F, O = w.shape
    b = w.reshape(F // 128, 128, O // 128, 128).transpose(2, 1, 0, 3)
    return _c(b.reshape(O, F))


def _sbnative(w):
    F, m = w.shape
    return _c(w.reshape(F // 128, 128, m).transpose(1, 0, 2).reshape(128, -1))


def _cols(v):
    return _c(np.asarray(v).reshape(-1, 128).T)


CST = np.zeros((128, 8), np.float32)
CST[:, 0] = 1.0
CST[:, 1] = EPS
CST[:, 2] = L1_SHIFT
ONES_CB = np.ones((128, 1), BF)
ONES_RB = np.ones((1, 128), BF)
ONES_CF = np.ones((128, 1), np.float32)
ONES_RF = np.ones((1, 128), np.float32)


class SemC:
    def __init__(self, h):
        self.h = h
        self.n = 0

    def inc(self, inst, k):
        inst.then_inc(self.h, k)
        self.n += k
        return inst


class LB:
    def __init__(self):
        self.nc = bass.Bass("TRN2", target_bir_lowering=False)
        self.es = ExitStack()
        self._names = set()

    def _nm(self, n):
        assert n not in self._names, n
        self._names.add(n)
        return n

    def inp(self, name, shape, dt=F32):
        return self.nc.dram_tensor(self._nm(name), list(shape), dt,
                                   kind="ExternalInput").ap()

    def out(self, name, shape, dt=F32):
        return self.nc.dram_tensor(self._nm(name), list(shape), dt,
                                   kind="ExternalOutput").ap()

    def tmp(self, name, shape, dt=F32):
        return self.nc.dram_tensor(self._nm(name), list(shape), dt,
                                   kind="Internal").ap()

    def sb(self, name, p, f, dt=F32):
        return self.es.enter_context(
            self.nc.sbuf_tensor(self._nm(name), [p, f], dt))

    def sbx(self, name, p, f, dt=F32):
        cm = self.nc.sbuf_tensor(self._nm(name), [p, f], dt)
        t = cm.__enter__()
        if not hasattr(self, "_open"):
            self._open = {}
        self._open[name] = cm
        return t

    def freex(self, name):
        self._open.pop(name).__exit__(None, None, None)

    def ps(self, name, p=128, f=512):
        return self.es.enter_context(
            self.nc.psum_tensor(self._nm(name), [p, f], F32))

    def sem(self, name):
        return SemC(self.es.enter_context(self.nc.semaphore(self._nm(name))))

    def block(self):
        return self.nc.Block()

    def done(self):
        if hasattr(self, "_open"):
            for name in reversed(list(self._open)):
                self._open.pop(name).__exit__(None, None, None)
        self.es.close()
        return self.nc



def emit_lnstats(lb, xtiles, sqbufs, psq, onescb, cm, cs, n_ft=FT):
    sq_base = cm.n
    cs_base = cs.n
    with lb.block() as blk:
        @blk.scalar
        def _(s):
            for i in range(n_ft):
                if i >= 2:
                    s.wait_ge(cs.h, cs_base + (i - 1))
                cm.inc(s.activation(sqbufs[i % 2][:, :], xtiles[i][:, :], AF.Square), 1)

        @blk.tensor
        def _(t):
            for i in range(n_ft):
                st0 = (i == 0)
                sp = (i == n_ft - 1)
                for tc in range(NTC):
                    t.matmul(psq[tc], onescb[:, :],
                             xtiles[i][:, tc * TC:(tc + 1) * TC], start=st0, stop=sp)
                t.wait_ge(cm.h, sq_base + i + 1)
                inst = None
                for tc in range(NTC):
                    inst = t.matmul(psq[2 + tc], onescb[:, :],
                                    sqbufs[i % 2][:, tc * TC:(tc + 1) * TC],
                                    start=st0, stop=sp)
                cs.inc(inst, 1)


def R_(rows, k, n=T):
    return rows[0:1, k * n:k * n + n]


def emit_rows(lb, psq, rows, scr, cst):
    with lb.block() as blk:
        @blk.scalar
        def _(s):
            for tc in range(NTC):
                sl = slice(tc * TC, (tc + 1) * TC)
                s.activation(R_(scr, 0)[:, sl], psq[tc], AF.Copy, scale=1.0 / D)
                s.activation(R_(scr, 1)[:, sl], psq[2 + tc], AF.Copy, scale=1.0 / D)
    with lb.block() as blk:
        @blk.vector
        def _(v):
            v.tensor_tensor(R_(scr, 2), R_(scr, 0), R_(scr, 0), op=ALU.mult)
            v.tensor_sub(R_(scr, 1), R_(scr, 1), R_(scr, 2))
    with lb.block() as blk:
        @blk.scalar
        def _(s):
            s.activation(R_(scr, 1), R_(scr, 1), AF.Ln, bias=cst[0:1, 1:2])
            s.activation(R_(rows, 0), R_(scr, 1), AF.Exp, scale=-0.5)
    with lb.block() as blk:
        @blk.vector
        def _(v):
            v.tensor_tensor(R_(rows, 1), R_(scr, 0), R_(rows, 0), op=ALU.mult)
            v.tensor_copy(R_(rows, 2), R_(scr, 0))


def emit_bc(lb, rows, rows_b, bc, onesrb, psum, cmB, nrows=2):
    with lb.block() as blk:
        @blk.vector
        def _(v):
            for r in range(nrows):
                v.tensor_copy(R_(rows_b, r), R_(rows, r))
    base = cmB.n
    with lb.block() as blk:
        @blk.tensor
        def _(t):
            for r in range(nrows):
                for tc in range(NTC):
                    i = r * NTC + tc
                    inst = t.matmul(psum[i][:, :], onesrb[:, :],
                                    R_(rows_b, r)[:, tc * TC:(tc + 1) * TC],
                                    start=True, stop=True)
                    cmB.inc(inst, 1)

        @blk.scalar
        def _(s):
            for r in range(nrows):
                for tc in range(NTC):
                    i = r * NTC + tc
                    s.wait_ge(cmB.h, base + i + 1)
                    s.activation(bc[:, (r * T) + tc * TC:(r * T) + (tc + 1) * TC],
                                 psum[i][:, :], AF.Copy)


def emit_apply(lb, xtiles, htiles, bc, n_ft=FT):
    with lb.block() as blk:
        @blk.vector
        def _(v):
            for i in range(n_ft):
                for tc in range(NTC):
                    sl = slice(tc * TC, (tc + 1) * TC)
                    v.tensor_tensor(htiles[i][:, sl], xtiles[i][:, sl],
                                    bc[:, tc * TC:(tc + 1) * TC], op=ALU.mult)
                    v.tensor_tensor(htiles[i][:, sl], htiles[i][:, sl],
                                    bc[:, T + tc * TC:T + (tc + 1) * TC],
                                    op=ALU.subtract)



def build_launch_A():
    lb = LB()
    x_in = lb.inp("x", (D, T))
    wq_in = lb.inp("wq", (128, FT * 12))
    wqk_in = lb.inp("wqk", (12, 128))
    wkv_in = lb.inp("wkv", (128, FT * 4))
    bq_in = lb.inp("bq", (12, 1))
    bqkt_in = lb.inp("bqkt", (4, 32))
    bkv_in = lb.inp("bkvbc", (128, 4))
    onescb_in = lb.inp("onescb", (128, 1), BF16)
    onesrb_in = lb.inp("onesrb", (1, 128), BF16)
    cst_in = lb.inp("cst", (128, 8))

    o_ckv = lb.out("ckv", (T, 4))
    o_qp = lb.out("qp", (4, 32 * T), BF16)

    xb = [lb.sb(f"xb{i}", 128, T, BF16) for i in range(FT)]
    hb = xb
    sq = [lb.sb(f"sq{i}", 128, T, BF16) for i in range(2)]
    wq_s = lb.sb("wq_s", 128, FT * 12, BF16)
    wqk_s = lb.sb("wqk_s", 12, 128, BF16)
    wkv_s = lb.sb("wkv_s", 128, FT * 4, BF16)
    bq_s = lb.sb("bq_s", 12, 1)
    bqkt_s = lb.sb("bqkt_s", 4, 32)
    bkv_s = lb.sb("bkv_s", 128, 4)
    onescb_s = lb.sb("onescb_s", 128, 1, BF16)
    onesrb_s = lb.sb("onesrb_s", 1, 128, BF16)
    cst_s = lb.sb("cst_s", 128, 8)
    rows = lb.sb("rows", 1, 3 * T)
    rows_b = lb.sb("rows_b", 1, 2 * T, BF16)
    scr4 = lb.sb("scr4", 1, 4 * T)
    bc = lb.sb("bc", 128, 2 * T, BF16)
    cq = lb.sb("cq", 12, T, BF16)
    ckv_s = lb.sb("ckv_s", 128, 32)
    qp_s = lb.sb("qp_s", 4, 32 * T, BF16)

    ps = [lb.ps(f"ps{i}") for i in range(6)]

    ld = lb.sem("ld")
    cm = lb.sem("cm")
    cs = lb.sem("cs")
    st = lb.sem("st")

    with lb.block() as blk:
        @blk.gpsimd
        def _(g):
            for i in range(FT):
                ld.inc(g.dma_start(out=xb[i][:, :], in_=x_in[i * 128:(i + 1) * 128, :]), 16)
            ld.inc(g.dma_start(out=wq_s[:, :], in_=wq_in), 16)
            ld.inc(g.dma_start(out=wqk_s[:, :], in_=wqk_in), 16)
            ld.inc(g.dma_start(out=wkv_s[:, :], in_=wkv_in), 16)

        @blk.sync
        def _(sy):
            for dst, src in ((bq_s, bq_in), (bqkt_s, bqkt_in), (bkv_s, bkv_in),
                             (cst_s, cst_in)):
                ld.inc(sy.dma_start(out=dst[:, :], in_=src), 16)

        @blk.scalar
        def _(s):
            ld.inc(s.dma_start(out=onescb_s[:, :], in_=onescb_in), 16)
            ld.inc(s.dma_start(out=onesrb_s[:, :], in_=onesrb_in), 16)
        ld_all = ld.n

        @blk.vector
        def _(v):
            v.wait_ge(ld.h, ld_all)

    psq4 = [ps[k][0:1, 0:TC] for k in range(4)]
    emit_lnstats(lb, xb, sq, psq4, onescb_s, cm, cs)
    emit_rows(lb, psq4, rows, scr4, cst_s)
    cmB = lb.sem("cmB")
    emit_bc(lb, rows, rows_b, bc, onesrb_s, ps, cmB)
    emit_apply(lb, xb, hb, bc)

    kmm = lb.sem("kmm")
    kev = lb.sem("kev")
    with lb.block() as blk:
        @blk.tensor
        def _(t):
            for tt in range(8):
                if tt >= 2:
                    t.wait_ge(kev.h, tt - 1)
                inst = None
                for i in range(FT):
                    inst = t.matmul(ps[tt % 2][0:128, 0:4],
                                    hb[i][:, tt * 128:(tt + 1) * 128],
                                    wkv_s[:, i * 4:(i + 1) * 4],
                                    start=(i == 0), stop=(i == FT - 1))
                kmm.inc(inst, 1)

        @blk.vector
        def _(v):
            for tt in range(8):
                v.wait_ge(kmm.h, tt + 1)
                kev.inc(v.scalar_tensor_tensor(ckv_s[:, tt * 4:(tt + 1) * 4],
                                               ps[tt % 2][0:128, 0:4], 1.0,
                                               bkv_s[:, :], op0=ALU.mult,
                                               op1=ALU.add), 1)

    cq_sem = lb.sem("cq_sem")
    with lb.block() as blk:
        @blk.tensor
        def _(t):
            inst = None
            for tc in range(NTC):
                for i in range(FT):
                    inst = t.matmul(ps[4 + tc][0:12, 0:TC],
                                    wq_s[:, i * 12:(i + 1) * 12],
                                    hb[i][:, tc * TC:(tc + 1) * TC],
                                    start=(i == 0), stop=(i == FT - 1))
            cq_sem.inc(inst, 1)

        @blk.vector
        def _(v):
            v.wait_ge(cq_sem.h, 1)
            for tc in range(NTC):
                v.tensor_scalar_add(cq[0:12, tc * TC:(tc + 1) * TC],
                                    ps[4 + tc][0:12, 0:TC], bq_s[0:12, 0:1])

    qp_mm = lb.sem("qp_mm")
    qp_ev = lb.sem("qp_ev")
    with lb.block() as blk:
        @blk.tensor
        def _(t):
            for h in range(32):
                for tc in range(NTC):
                    i = h * NTC + tc
                    if i >= 2:
                        t.wait_ge(qp_ev.h, i - 1)
                    inst = t.matmul(ps[i % 2][0:4, 0:TC],
                                    wqk_s[:, 4 * h:4 * h + 4],
                                    cq[:, tc * TC:(tc + 1) * TC],
                                    start=True, stop=True)
                    qp_mm.inc(inst, 1)

        @blk.vector
        def _(v):
            for h in range(32):
                for tc in range(NTC):
                    i = h * NTC + tc
                    v.wait_ge(qp_mm.h, i + 1)
                    inst = v.tensor_scalar_add(
                        qp_s[0:4, h * T + tc * TC:h * T + (tc + 1) * TC],
                        ps[i % 2][0:4, 0:TC], bqkt_s[0:4, h:h + 1])
                    qp_ev.inc(inst, 1)

    with lb.block() as blk:
        @blk.sync
        def _(sy):
            for tt in range(8):
                st.inc(sy.dma_start(out=o_ckv[tt * 128:(tt + 1) * 128, :],
                                    in_=ckv_s[:, tt * 4:(tt + 1) * 4]), 16)
            st.inc(sy.dma_start(out=o_qp, in_=qp_s[:, :]), 16)
            sy.wait_ge(st.h, st.n)

    lb.done()
    return lb.nc




def build_launch_B():
    lb = LB()
    nc = lb.nc
    x_in = lb.inp("x", (D, T))
    qp_in = lb.inp("qp", (4, 32 * T), BF16)
    ckvf_in = lb.inp("ckvf", (4, D))
    ckva_in = lb.inp("ckva", (128, 5 * FT))
    wvut_in = lb.inp("wvut", (128, 128))
    wob_in = lb.inp("wob", (D, D))
    w2f_in = lb.inp("w2f", (3 * D, D))
    bo_in = lb.inp("boc", (128, 32))
    bv_in = lb.inp("bvc", (128, 32))
    p32_in = lb.inp("p32", (32, 128))
    qm_in = lb.inp("qmask", (1, T))
    onescb_in = lb.inp("onescb", (128, 1), BF16)
    onesrb_in = lb.inp("onesrb", (1, 128), BF16)
    cst_in = lb.inp("cst", (128, 8))

    o_x1 = lb.out("x1", (D, T))
    o_r1 = lb.out("r1", (1, 2 * T))
    o_qp = lb.out("qpart", (128, 32))
    o_s2 = lb.out("s2", (1, 1))
    o_w2b = lb.out("w2bf", (3 * D, D), BF16)

    a_d = nc.dram_tensor("ad", [128, T], BF16, kind="Internal").ap()
    r_d = nc.dram_tensor("rd", [32, T], BF16, kind="Internal").ap()
    m_d = nc.dram_tensor("md", [128, D], BF16, kind="Internal").ap()

    ckvf_s = lb.sb("ckvf_s", 36, D, BF16)
    ckva_s = lb.sb("ckva_s", 128, 5 * FT, BF16)
    wvut_s = lb.sb("wvut_s", 128, 128, BF16)
    wo_b = [lb.sb(f"wo_b{j}", 128, D, BF16) for j in range(2)]
    bo_s = lb.sb("bo_s", 128, 32)
    bv_s = lb.sb("bv_s", 128, 32, BF16)
    p32_s = lb.sb("p32_s", 32, 128, BF16)
    qm_s = lb.sb("qm_s", 1, T)
    onescb_s = lb.sb("onescb_s", 128, 1, BF16)
    onesrb_s = lb.sb("onesrb_s", 1, 128, BF16)
    cst_s = lb.sb("cst_s", 128, 8)
    biasL0 = lb.sb("biasL0", 128, 32)
    biasP = lb.sb("biasP", 128, 32 * 32)
    mstage = [lb.sb(f"mstage{j}", 4, D, BF16) for j in range(2)]
    qp2b = [lb.sb(f"qp2b{j}", 36, T, BF16) for j in range(2)]
    expQ = [lb.sb(f"expQ{j}", 128, 2 * TC, BF16) for j in range(4)]
    w5h = [lb.sb(f"w5h{j}", 5, T, BF16) for j in range(2)]
    a128 = lb.sb("a128", 128, T, BF16)
    r32 = lb.sb("r32", 32, T, BF16)
    m128 = lb.sb("m128", 128, D, BF16)
    rbc = lb.sb("rbc", 128, T)
    ao = [lb.sb(f"ao{i}", 128, T, BF16) for i in range(FT)]
    sqT = [lb.sb(f"sqT{i}", 128, T, BF16) for i in range(2)]
    sq = [lb.sb(f"sq{i}", 128, TC, BF16) for i in range(2)]
    qp_sb = lb.sb("qp_sb", 128, 32)
    rows = lb.sb("rows", 1, 3 * T)
    rows_b = lb.sb("rows_b", 1, 2 * T, BF16)
    scr4 = lb.sb("scr4", 1, 4 * T)
    bc = lb.sb("bc", 128, 2 * T, BF16)
    xst = [lb.sb(f"xst{j}", 128, TC) for j in range(2)]
    t1 = lb.sb("t1", 128, TC, BF16)
    t2 = lb.sb("t2", 128, TC, BF16)
    x1t = [lb.sb(f"x1t{j}", 128, TC) for j in range(2)]
    x1b = [lb.sb(f"x1b{j}", 128, T, BF16) for j in range(2)]
    wbc = lb.sb("wbc", 128, T, BF16)

    ps = [lb.ps(f"ps{i}") for i in range(4)]
    ps_sc = [lb.ps(f"ps_sc{j}", 128, 2 * TC) for j in range(2)]

    ld = lb.sem("ld")
    st = lb.sem("st")

    with lb.block() as blk:
        @blk.gpsimd
        def _(g):
            ld.inc(g.dma_start(out=ckvf_s[0:4, :], in_=ckvf_in), 16)
            ld.inc(g.dma_start(out=ckvf_s[32:36, :], in_=ckvf_in), 16)
            ld.inc(g.dma_start(out=ckva_s[:, :], in_=ckva_in), 16)
            ld.inc(g.dma_start(out=wvut_s[:, :], in_=wvut_in), 16)
            ld.inc(g.dma_start(out=bv_s[:, :], in_=bv_in), 16)
            ld.inc(g.dma_start(out=p32_s[:, :], in_=p32_in), 16)

        @blk.sync
        def _(sy):
            ld.inc(sy.dma_start(out=bo_s[:, :], in_=bo_in), 16)
            ld.inc(sy.dma_start(out=qm_s[:, :], in_=qm_in), 16)
            ld.inc(sy.dma_start(out=cst_s[:, :], in_=cst_in), 16)

        @blk.scalar
        def _(s):
            ld.inc(s.dma_start(out=onescb_s[:, :], in_=onescb_in), 16)
            ld.inc(s.dma_start(out=onesrb_s[:, :], in_=onesrb_in), 16)
        ld_all = ld.n

        @blk.vector
        def _(v):
            v.wait_ge(ld.h, ld_all)

    wol = [lb.sem("wol0"), lb.sem("wol1")]
    bmm = lb.sem("bmm")
    bev = lb.sem("bev")
    mmm = lb.sem("mmm")
    mev = lb.sem("mev")
    mst = lb.sem("mst")
    with lb.block() as blk:
        @blk.gpsimd
        def _(g):
            for oc in range(FT):
                if oc >= 2:
                    g.wait_ge(bmm.h, oc - 1)
                    g.wait_ge(mmm.h, 8 * (oc - 1))
                wol[oc % 2].inc(
                    g.dma_start(out=wo_b[oc % 2][:, :],
                                in_=wob_in[oc * 128:(oc + 1) * 128, :]), 16)

        @blk.tensor
        def _(t):
            for oc in range(FT):
                t.wait_ge(wol[oc % 2].h, 16 * (oc // 2 + 1))
                if oc >= 2:
                    t.wait_ge(bev.h, oc - 1)
                inst = None
                for c in range(FT):
                    inst = t.matmul(ps[oc % 2][0:128, c:c + 1],
                                    wo_b[oc % 2][:, c * 128:(c + 1) * 128],
                                    bv_s[:, oc:oc + 1],
                                    start=True, stop=True)
                bmm.inc(inst, 1)
                for c8 in range(8):
                    i = oc * 8 + c8
                    if i >= 2:
                        t.wait_ge(mev.h, i - 1)
                    inst = t.matmul(ps[2 + (i % 2)][0:4, 0:TC],
                                    wvut_s[:, 4 * oc:4 * oc + 4],
                                    wo_b[oc % 2][:, c8 * TC:(c8 + 1) * TC],
                                    start=True, stop=True)
                    mmm.inc(inst, 1)

        @blk.vector
        def _(v):
            for oc in range(FT):
                v.wait_ge(bmm.h, oc + 1)
                bev.inc(v.tensor_copy(biasP[:, oc * 32:(oc + 1) * 32],
                                      ps[oc % 2][0:128, 0:32]), 1)
                for c8 in range(8):
                    i = oc * 8 + c8
                    v.wait_ge(mmm.h, i + 1)
                    if oc >= 2 and c8 == 0:
                        v.wait_ge(mst.h, 16 * (oc - 1))
                    mev.inc(v.tensor_copy(mstage[oc % 2][:, c8 * TC:(c8 + 1) * TC],
                                          ps[2 + (i % 2)][0:4, 0:TC]), 1)

        @blk.sync
        def _(sy):
            for oc in range(FT):
                sy.wait_ge(mev.h, (oc + 1) * 8)
                mst.inc(sy.dma_start(out=m_d[oc * 4:(oc + 1) * 4, :],
                                     in_=mstage[oc % 2][:, :]), 16)

    with lb.block() as blk:
        @blk.vector
        def _(v):
            v.tensor_reduce(biasL0[:, :],
                            biasP.rearrange("p (o c) -> p c o", c=32),
                            axis=AXX, op=ALU.add)
    with lb.block() as blk:
        @blk.vector
        def _(v):
            v.tensor_add(biasL0[:, :], biasL0[:, :], bo_s[:, :])

    qpl = [lb.sem("qpl0"), lb.sem("qpl1")]
    smm = lb.sem("smm")
    sex = lb.sem("sex")
    ctxd = lb.sem("ctxd")
    cev = lb.sem("cev")
    w5s = lb.sem("w5s")
    seqB = [(p, kt) for p in range(16) for kt in range(FT)]
    with lb.block() as blk:
        @blk.sync
        def _(sy):
            for p in range(16):
                if p >= 2:
                    sy.wait_ge(smm.h, 2 * FT * (p - 1))
                qpl[p % 2].inc(
                    sy.dma_start(out=qp2b[p % 2][0:4, :],
                                 in_=qp_in[0:4, (2 * p) * T:(2 * p + 1) * T]), 16)
                qpl[p % 2].inc(
                    sy.dma_start(out=qp2b[p % 2][32:36, :],
                                 in_=qp_in[0:4, (2 * p + 1) * T:(2 * p + 2) * T]), 16)

        @blk.tensor
        def _(t):
            for idx in range(len(seqB) + 1):
                if idx < len(seqB):
                    p, kt = seqB[idx]
                    if kt == 0:
                        t.wait_ge(qpl[p % 2].h, 32 * (p // 2 + 1))
                    ck0 = ckvf_s[0:4, kt * 128:(kt + 1) * 128]
                    ck1 = ckvf_s[32:36, kt * 128:(kt + 1) * 128]
                    q2 = qp2b[p % 2]
                    if idx >= 1:
                        t.wait_ge(sex.h, 2 * idx - 1)
                    t.matmul(ps_sc[0][0:128, 0:TC], ck0, q2[0:4, 0:TC],
                             start=True, stop=True, tile_position=(0, 0))
                    smm.inc(t.matmul(ps_sc[0][0:128, TC:2 * TC], ck1,
                                     q2[32:36, 0:TC], start=True, stop=True,
                                     tile_position=(32, 0)), 1)
                    if idx >= 1:
                        t.wait_ge(sex.h, 2 * idx)
                    t.matmul(ps_sc[1][0:128, 0:TC], ck0, q2[0:4, TC:2 * TC],
                             start=True, stop=True, tile_position=(0, 0))
                    smm.inc(t.matmul(ps_sc[1][0:128, TC:2 * TC], ck1,
                                     q2[32:36, TC:2 * TC], start=True, stop=True,
                                     tile_position=(32, 0)), 1)
                if idx >= 1:
                    p1, kt1 = seqB[idx - 1]
                    if kt1 == 0 and p1 >= 1:
                        t.wait_ge(cev.h, 4 * p1)
                    if idx == len(seqB):
                        t.wait_ge(sex.h, 2 * idx)
                    ckv5 = ckva_s[:, kt1 * 5:(kt1 + 1) * 5]
                    e0 = expQ[(2 * (idx - 1)) % 4]
                    e1 = expQ[(2 * (idx - 1) + 1) % 4]
                    st0, sp1 = (kt1 == 0), (kt1 == FT - 1)
                    t.matmul(ps[0][0:5, 0:TC], ckv5, e0[:, 0:TC],
                             start=st0, stop=sp1, tile_position=(0, 0))
                    t.matmul(ps[1][32:37, 0:TC], ckv5, e0[:, TC:2 * TC],
                             start=st0, stop=sp1, tile_position=(0, 32))
                    t.matmul(ps[2][0:5, 0:TC], ckv5, e1[:, 0:TC],
                             start=st0, stop=sp1, tile_position=(0, 0))
                    ctxd.inc(t.matmul(ps[3][32:37, 0:TC], ckv5,
                                      e1[:, TC:2 * TC],
                                      start=st0, stop=sp1,
                                      tile_position=(0, 32)), 1)

        @blk.scalar
        def _(s):
            for idx in range(len(seqB)):
                for qc in range(2):
                    j = 2 * idx + qc
                    s.wait_ge(smm.h, j + 1)
                    if j >= 4:
                        s.wait_ge(ctxd.h, (j - 4) // 2 + 1)
                    sex.inc(s.activation(expQ[j % 4][:, :],
                                         ps_sc[qc][0:128, 0:2 * TC],
                                         AF.Exp, scale=0.5), 1)

        @blk.vector
        def _(v):
            for p in range(16):
                v.wait_ge(ctxd.h, (p + 1) * FT)
                if p >= 1:
                    v.wait_ge(w5s.h, 64 * p)
                cev.inc(v.tensor_copy(w5h[0][0:5, 0:TC], ps[0][0:5, 0:TC]), 1)
                cev.inc(v.tensor_copy(w5h[0][0:5, TC:T], ps[2][0:5, 0:TC]), 1)
                cev.inc(v.tensor_copy(w5h[1][0:5, 0:TC], ps[1][32:37, 0:TC]), 1)
                cev.inc(v.tensor_copy(w5h[1][0:5, TC:T], ps[3][32:37, 0:TC]), 1)

        @blk.gpsimd
        def _(g):
            for p in range(16):
                for j in range(6):
                    k = p * 6 + j
                    st.inc(g.dma_start(
                        out=o_w2b[k * 128:(k + 1) * 128, :],
                        in_=w2f_in[k * 128:(k + 1) * 128, :]), 16)
                g.wait_ge(cev.h, (p + 1) * 4)
                g.dma_start(out=a_d[(2 * p) * 4:(2 * p + 1) * 4, :],
                            in_=w5h[0][0:4, :]).then_inc(w5s.h, 16)
                g.dma_start(out=r_d[2 * p:2 * p + 1, :],
                            in_=w5h[0][4:5, :]).then_inc(w5s.h, 16)
                g.dma_start(out=a_d[(2 * p + 1) * 4:(2 * p + 2) * 4, :],
                            in_=w5h[1][0:4, :]).then_inc(w5s.h, 16)
                g.dma_start(out=r_d[2 * p + 1:2 * p + 2, :],
                            in_=w5h[1][4:5, :]).then_inc(w5s.h, 16)
            w5s.n += 16 * 64

    with lb.block() as blk:
        @blk.sync
        def _(sy):
            sy.wait_ge(w5s.h, 32 * 32)
            sy.wait_ge(mst.h, 16 * 32)
            st.inc(sy.dma_start(out=a128[:, :], in_=a_d), 16)
            st.inc(sy.dma_start(out=r32[:, :], in_=r_d), 16)
            st.inc(sy.dma_start(out=m128[:, :], in_=m_d), 16)
            sy.wait_ge(st.h, st.n)

    b5 = lb.sem("b5")
    with lb.block() as blk:
        @blk.tensor
        def _(t):
            for qc in range(NTC):
                b5.inc(t.matmul(ps[qc][0:128, 0:TC], p32_s[:, :],
                                r32[:, qc * TC:(qc + 1) * TC],
                                start=True, stop=True), 1)

        @blk.vector
        def _(v):
            v.wait_ge(b5.h, 2)
            for qc in range(NTC):
                b5.inc(v.reciprocal(rbc[:, qc * TC:(qc + 1) * TC],
                                    ps[qc][0:128, 0:TC]), 1)
    with lb.block() as blk:
        @blk.vector
        def _(v):
            for qc in range(NTC):
                v.tensor_tensor(a128[:, qc * TC:(qc + 1) * TC],
                                a128[:, qc * TC:(qc + 1) * TC],
                                rbc[:, qc * TC:(qc + 1) * TC], op=ALU.mult)

    omm = lb.sem("omm")
    oev = lb.sem("oev")
    with lb.block() as blk:
        @blk.tensor
        def _(t):
            for oc in range(FT):
                for qc in range(NTC):
                    i = oc * NTC + qc
                    if i >= 2:
                        t.wait_ge(oev.h, i - 1)
                    omm.inc(t.matmul(ps[i % 2][0:128, 0:TC],
                                     m128[:, oc * 128:(oc + 1) * 128],
                                     a128[:, qc * TC:(qc + 1) * TC],
                                     start=True, stop=True), 1)

        @blk.vector
        def _(v):
            for oc in range(FT):
                for qc in range(NTC):
                    i = oc * NTC + qc
                    v.wait_ge(omm.h, i + 1)
                    oev.inc(v.tensor_scalar_add(ao[oc][:, qc * TC:(qc + 1) * TC],
                                                ps[i % 2][0:128, 0:TC],
                                                biasL0[:, oc:oc + 1]), 1)

    cm2 = lb.sem("cm2")
    cs2 = lb.sem("cs2")
    psq4b = [ps_sc[0][0:1, 0:TC], ps_sc[0][0:1, TC:2 * TC],
             ps_sc[1][0:1, 0:TC], ps_sc[1][0:1, TC:2 * TC]]
    emit_lnstats(lb, ao, sqT, psq4b, onescb_s, cm2, cs2)
    emit_rows(lb, psq4b, rows, scr4, cst_s)
    cmB2 = lb.sem("cmB2")
    emit_bc(lb, rows, rows_b, bc, onesrb_s, ps, cmB2)

    xl = [lb.sem("xl0"), lb.sem("xl1")]
    xf = lb.sem("xf")
    dv8 = lb.sem("dv8")
    sq8 = lb.sem("sq8")
    st8 = lb.sem("st8")
    mm8 = lb.sem("mm8")
    with lb.block() as blk:
        @blk.sync
        def _(sy):
            for oc in range(FT):
                for qc in range(NTC):
                    i = oc * NTC + qc
                    if i >= 2:
                        sy.wait_ge(xf.h, i - 1)
                    xl[i % 2].inc(sy.dma_start(
                        out=xst[i % 2][:, :],
                        in_=x_in[oc * 128:(oc + 1) * 128, qc * TC:(qc + 1) * TC]), 16)

        @blk.vector
        def _(v):
            for oc in range(FT):
                for qc in range(NTC):
                    i = oc * NTC + qc
                    sl = slice(qc * TC, (qc + 1) * TC)
                    v.wait_ge(xl[i % 2].h, 16 * (i // 2 + 1))
                    if i >= 2:
                        v.wait_ge(st8.h, 16 * (i - 1))
                        v.wait_ge(mm8.h, (i - 1))
                    v.tensor_tensor(t1[:, :], ao[oc][:, sl], bc[:, sl], op=ALU.mult)
                    v.tensor_tensor(t2[:, :], t1[:, :], bc[:, T + qc * TC:T + (qc + 1) * TC],
                                    op=ALU.subtract)
                    xf.inc(v.tensor_tensor(x1t[i % 2][:, :], xst[i % 2][:, :],
                                           t2[:, :], op=ALU.add), 1)
                    dv8.inc(v.tensor_copy(x1b[i % 2][:, qc * TC:(qc + 1) * TC],
                                          x1t[i % 2][:, :]), 1)

        @blk.scalar
        def _(s):
            for oc in range(FT):
                for qc in range(NTC):
                    i = oc * NTC + qc
                    s.wait_ge(dv8.h, i + 1)
                    if i >= 2:
                        s.wait_ge(mm8.h, i - 1)
                    sq8.inc(s.activation(sq[i % 2][:, :],
                                         x1b[i % 2][:, qc * TC:(qc + 1) * TC],
                                         AF.Square), 1)

        @blk.tensor
        def _(t):
            for oc in range(FT):
                for qc in range(NTC):
                    i = oc * NTC + qc
                    t.wait_ge(dv8.h, i + 1)
                    t.matmul(ps[2 + qc][0:1, 0:TC], onescb_s[:, :],
                             x1b[i % 2][:, qc * TC:(qc + 1) * TC],
                             start=(oc == 0), stop=(oc == FT - 1))
                    t.wait_ge(sq8.h, i + 1)
                    mm8.inc(t.matmul(ps_sc[qc][0:1, 0:TC], onescb_s[:, :],
                                     sq[i % 2][:, :],
                                     start=(oc == 0), stop=(oc == FT - 1)), 1)

        @blk.gpsimd
        def _(g):
            for oc in range(FT):
                for qc in range(NTC):
                    i = oc * NTC + qc
                    g.wait_ge(xf.h, i + 1)
                    st8.inc(g.dma_start(
                        out=o_x1[oc * 128:(oc + 1) * 128, qc * TC:(qc + 1) * TC],
                        in_=x1t[i % 2][:, :]), 16)

    emit_rows(lb, [ps[2][0:1, 0:TC], ps[3][0:1, 0:TC],
                   ps_sc[0][0:1, 0:TC], ps_sc[1][0:1, 0:TC]], rows, scr4, cst_s)

    with lb.block() as blk:
        @blk.vector
        def _(v):
            v.tensor_tensor(R_(scr4, 0), qm_s[0:1, 0:T], R_(rows, 0), op=ALU.mult)
            v.tensor_tensor(R_(scr4, 1), qm_s[0:1, 0:T], R_(rows, 1), op=ALU.mult)
    with lb.block() as blk:
        @blk.vector
        def _(v):
            v.tensor_reduce(scr4[0:1, 3 * T:3 * T + 1], R_(scr4, 1), axis=AXX, op=ALU.add)
            v.tensor_copy(R_(rows_b, 0), R_(scr4, 0))
    b9 = lb.sem("b9")
    with lb.block() as blk:
        @blk.tensor
        def _(t):
            for qc in range(NTC):
                b9.inc(t.matmul(ps[qc][0:128, 0:TC], onesrb_s[:, :],
                                R_(rows_b, 0)[:, qc * TC:(qc + 1) * TC],
                                start=True, stop=True), 1)

        @blk.scalar
        def _(s):
            s.wait_ge(b9.h, 2)
            for qc in range(NTC):
                s.activation(wbc[:, qc * TC:(qc + 1) * TC], ps[qc][0:128, 0:TC],
                             AF.Copy)
    x2l = [lb.sem("x2l0"), lb.sem("x2l1")]
    x2f = lb.sem("x2f")
    with lb.block() as blk:
        @blk.gpsimd
        def _(g):
            for i in range(FT):
                if i >= 2:
                    g.wait_ge(x2f.h, i - 1)
                x2l[i % 2].inc(g.dma_start(out=x1b[i % 2][:, :],
                                           in_=o_x1[i * 128:(i + 1) * 128, :]), 16)

        @blk.vector
        def _(v):
            for i in range(FT):
                v.wait_ge(x2l[i % 2].h, 16 * (i // 2 + 1))
                inst = v.scalar_tensor_tensor(sqT[0][:, :], x1b[i % 2][:, :], 1.0,
                                              wbc[:, :], op0=ALU.mult, op1=ALU.mult,
                                              accum_out=qp_sb[:, i:i + 1])
                x2f.inc(inst, 1)

    with lb.block() as blk:
        @blk.sync
        def _(sy):
            st.inc(sy.dma_start(out=o_r1, in_=rows[0:1, 0:2 * T]), 16)
            st.inc(sy.dma_start(out=o_qp, in_=qp_sb[:, :]), 16)
            st.inc(sy.dma_start(out=o_s2, in_=scr4[0:1, 3 * T:3 * T + 1]), 16)
            sy.wait_ge(st.h, st.n)

    lb.done()
    return nc



def build_launch_C(upto=99):
    lb = LB()
    nc = lb.nc
    x1_in = lb.inp("x1", (D, T))
    r1_in = lb.inp("r1", (1, 2 * T))
    qpall_in = lb.inp("qpall", (128, 128))
    s2all_in = lb.inp("s2all", (1, 4))
    wt_in = lb.inp("wtb", (D, D))
    bc_in = lb.inp("bcol", (128, 32))
    at_in = lb.inp("atsb", (128, D))
    bt_in = lb.inp("btn", (128, D))
    rwt_in = lb.inp("rwt", (128, 32 * 8))
    rb_in = lb.inp("rb8", (1, 8))
    e8_in = lb.inp("e8", (8, 128))
    qk_in = lb.inp("qkb", (2 * D, D))
    vw_in = lb.inp("vwb", (16 * 128, 32 * 256))
    qkb_in = lb.inp("qkbc", (128, 64))
    vb_in = lb.inp("vbbc", (128, D))
    onescb_in = lb.inp("onescb", (128, 1), BF16)
    onesrb_in = lb.inp("onesrb", (1, 128), BF16)
    onesrf_in = lb.inp("onesrf", (1, 128))
    cst_in = lb.inp("cst", (128, 8))

    o_x2 = lb.out("x2", (D, T))
    o_qf = lb.out("qf", (D, T), BF16)
    o_kf = lb.out("kf", (D, T), BF16)
    o_vt = lb.out("vt", (T, D), BF16)

    hb = [lb.sb(f"hb{i}", 128, T, BF16) for i in range(FT)]
    f0st = [lb.sb(f"f0st{j}", 128, TC, BF16) for j in range(2)]
    f0s = [lb.sb(f"f0s{j}", 128, T, BF16) for j in range(2)]
    rows = lb.sb("rows", 1, 3 * T)
    rows_b = lb.sb("rows_b", 1, 2 * T, BF16)
    scr4 = lb.sb("scr4", 1, 3 * T + 16)
    bc = lb.sb("bc", 128, 2 * T, BF16)
    wtb = [lb.sb(f"wtb{j}", 128, D, BF16) for j in range(2)]
    xs2 = [lb.sb(f"xs2{j}", 128, T, BF16) for j in range(2)]
    xst = [lb.sb(f"xst{j}", 128, TC) for j in range(2)]
    t1 = lb.sb("t1", 128, TC, BF16)
    t2 = lb.sb("t2", 128, TC, BF16)
    x2t = [lb.sb(f"x2t{j}", 128, TC) for j in range(2)]
    sq = [lb.sb(f"sq{i}", 128, TC, BF16) for i in range(2)]
    tsc = lb.sb("tsc", 128, T, BF16)
    at_s = lb.sb("at_s", 128, D, BF16)
    bt_s = lb.sb("bt_s", 128, D, BF16)
    bcol_s = lb.sb("bcol_s", 128, 32)
    rwt_s = lb.sb("rwt_s", 128, 32 * 8)
    rb_s = lb.sb("rb_s", 1, 8)
    e8_s = lb.sb("e8_s", 8, 128)
    qpall_s = lb.sb("qpall_s", 128, 128)
    s2_s = lb.sb("s2_s", 1, 8)
    qvec = lb.sb("qvec", 128, 32)
    s2col = lb.sb("s2col", 128, 1)
    lrow = lb.sb("lrow", 1, 16)
    r128 = lb.sb("r128", 128, 1)
    rcol = lb.sb("rcol", 8, 1)
    qkbc_s = lb.sb("qkbc_s", 128, 64)
    vbbc_s = lb.sb("vbbc_s", 128, D, BF16)
    qko = [lb.sb(f"qko{j}", 128, TC, BF16) for j in range(2)]
    vto = [lb.sb(f"vto{j}", 128, 256, BF16) for j in range(2)]
    onescb_s = lb.sb("onescb_s", 128, 1, BF16)
    onesrb_s = lb.sb("onesrb_s", 1, 128, BF16)
    onesrf_s = lb.sb("onesrf_s", 1, 128)
    cst_s = lb.sb("cst_s", 128, 8)

    rt_d = nc.dram_tensor("rtd", [1, 8], F32, kind="Internal").ap()
    f0_d = nc.dram_tensor("f0d", [D, T], BF16, kind="Internal").ap()

    ps = [lb.ps(f"ps{i}") for i in range(6)]
    ld = lb.sem("ld")
    st = lb.sem("st")

    with lb.block() as blk:
        @blk.gpsimd
        def _(g):
            ld.inc(g.dma_start(out=at_s[:, :], in_=at_in), 16)
            ld.inc(g.dma_start(out=bt_s[:, :], in_=bt_in), 16)
            ld.inc(g.dma_start(out=vbbc_s[:, :], in_=vb_in), 16)

        @blk.sync
        def _(sy):
            for dst, srcin in ((rows[0:1, 0:2 * T], r1_in), (qpall_s[:, :], qpall_in),
                               (s2_s[0:1, 0:4], s2all_in), (bcol_s[:, :], bc_in),
                               (rwt_s[:, :], rwt_in), (rb_s[:, :], rb_in),
                               (e8_s[:, :], e8_in), (qkbc_s[:, :], qkb_in),
                               (onesrf_s[:, :], onesrf_in),
                               (cst_s[:, :], cst_in)):
                ld.inc(sy.dma_start(out=dst, in_=srcin), 16)

        @blk.scalar
        def _(s):
            ld.inc(s.dma_start(out=onescb_s[:, :], in_=onescb_in), 16)
            ld.inc(s.dma_start(out=onesrb_s[:, :], in_=onesrb_in), 16)
        ld_all = ld.n

        @blk.vector
        def _(v):
            v.wait_ge(ld.h, ld_all)

    if upto < 2:
        lb.done()
        return nc
    c2 = lb.sem("c2")
    with lb.block() as blk:
        @blk.vector
        def _(v):
            v.tensor_add(qvec[:, :], qpall_s[:, 0:32], qpall_s[:, 32:64])
            v.tensor_add(qvec[:, :], qvec[:, :], qpall_s[:, 64:96])
            v.tensor_add(qvec[:, :], qvec[:, :], qpall_s[:, 96:128])
            v.tensor_reduce(s2_s[0:1, 4:5], s2_s[0:1, 0:4], axis=AXX, op=ALU.add)
    with lb.block() as blk:
        @blk.tensor
        def _(t):
            c2.inc(t.matmul(ps[0][0:128, 0:1], onesrf_s[:, :], s2_s[0:1, 4:5],
                            start=True, stop=True), 1)

        @blk.scalar
        def _(s):
            s.wait_ge(c2.h, 1)
            s.activation(s2col[:, :], ps[0][0:128, 0:1], AF.Copy)
    with lb.block() as blk:
        @blk.vector
        def _(v):
            v.tensor_scalar(qvec[:, :], qvec[:, :], s2col[:, 0:1], None,
                            op0=ALU.subtract)
    with lb.block() as blk:
        @blk.tensor
        def _(t):
            inst = None
            for fc in range(FT):
                inst = t.matmul(ps[1][0:1, 0:8], qvec[:, fc:fc + 1],
                                rwt_s[:, fc * 8:(fc + 1) * 8],
                                start=(fc == 0), stop=(fc == FT - 1))
            c2.inc(inst, 1)

        @blk.vector
        def _(v):
            v.wait_ge(c2.h, 2)
            v.scalar_tensor_tensor(lrow[0:1, 0:8], ps[1][0:1, 0:8], 1.0 / NQ,
                                   rb_s[0:1, 0:8], op0=ALU.mult, op1=ALU.add)
    with lb.block() as blk:
        @blk.vector
        def _(v):
            v.tensor_reduce(lrow[0:1, 8:9], lrow[0:1, 0:8], axis=AXX, op=ALU.max,
                            negate=True)
    with lb.block() as blk:
        @blk.scalar
        def _(s):
            s.activation(lrow[0:1, 0:8], lrow[0:1, 0:8], AF.Exp,
                         bias=lrow[0:1, 8:9], accum_out=lrow[0:1, 9:10])
    with lb.block() as blk:
        @blk.vector
        def _(v):
            v.reciprocal(lrow[0:1, 10:11], lrow[0:1, 9:10])
    with lb.block() as blk:
        @blk.vector
        def _(v):
            v.tensor_scalar(lrow[0:1, 0:8], lrow[0:1, 0:8], lrow[0:1, 10:11],
                            None, op0=ALU.mult)
    with lb.block() as blk:
        @blk.vector
        def _(v):
            v.tensor_scalar(lrow[0:1, 0:8], lrow[0:1, 0:8], 2.0, None,
                            op0=ALU.mult)
    with lb.block() as blk:
        @blk.sync
        def _(sy):
            st.inc(sy.dma_start(out=rt_d, in_=lrow[0:1, 0:8]), 16)
            sy.wait_ge(st.h, st.n)
    with lb.block() as blk:
        @blk.sync
        def _(sy):
            st.inc(sy.dma_start(out=rcol[:, :], in_=rt_d.rearrange("a e -> e a")), 16)
            sy.wait_ge(st.h, st.n)
    with lb.block() as blk:
        @blk.tensor
        def _(t):
            c2.inc(t.matmul(ps[2][0:128, 0:1], e8_s[0:8, 0:128],
                            rcol[:, :], start=True, stop=True), 1)

        @blk.scalar
        def _(s):
            s.wait_ge(c2.h, 3)
            s.activation(r128[:, :], ps[2][0:128, 0:1], AF.Copy)

    if upto < 3:
        lb.done()
        return nc
    c3l = [lb.sem("c3l0"), lb.sem("c3l1")]
    c3f = lb.sem("c3f")
    emit_bc(lb, rows, rows_b, bc, onesrb_s, ps, lb.sem("cmBc3"))
    with lb.block() as blk:
        @blk.gpsimd
        def _(g):
            for i in range(FT):
                if i >= 2:
                    g.wait_ge(c3f.h, i - 1)
                c3l[i % 2].inc(g.dma_start(out=xs2[i % 2][:, :],
                                           in_=x1_in[i * 128:(i + 1) * 128, :]), 16)

        @blk.vector
        def _(v):
            for i in range(FT):
                v.wait_ge(c3l[i % 2].h, 16 * (i // 2 + 1))
                for tc in range(NTC):
                    sl = slice(tc * TC, (tc + 1) * TC)
                    v.tensor_tensor(hb[i][:, sl], xs2[i % 2][:, sl],
                                    bc[:, tc * TC:(tc + 1) * TC], op=ALU.mult)
                inst = None
                for tc in range(NTC):
                    sl = slice(tc * TC, (tc + 1) * TC)
                    inst = v.tensor_tensor(hb[i][:, sl], hb[i][:, sl],
                                           bc[:, T + tc * TC:T + (tc + 1) * TC],
                                           op=ALU.subtract)
                c3f.inc(inst, 1)

    if upto < 4:
        lb.done()
        return nc
    c4 = lb.sem("c4")
    with lb.block() as blk:
        @blk.tensor
        def _(t):
            inst = None
            for tc in range(NTC):
                for fc in range(FT):
                    inst = t.matmul(ps[4 + tc][0:128, 0:TC],
                                    at_s[:, fc * 128:(fc + 1) * 128],
                                    hb[fc][:, tc * TC:(tc + 1) * TC],
                                    start=(fc == 0), stop=(fc == FT - 1))
            c4.inc(inst, 1)

        @blk.vector
        def _(v):
            v.wait_ge(c4.h, 1)
            for tc in range(NTC):
                v.tensor_scalar(tsc[:, tc * TC:(tc + 1) * TC],
                                ps[4 + tc][0:128, 0:TC], r128[:, 0:1], None,
                                op0=ALU.mult)

    if upto < 5:
        lb.done()
        return nc
    wl = [lb.sem("wl0"), lb.sem("wl1")]
    fmm = lb.sem("fmm")
    fev = lb.sem("fev")
    fsq = lb.sem("fsq")
    fsm = lb.sem("fsm")
    fst = lb.sem("fst")
    NI = FT * NTC
    with lb.block() as blk:
        @blk.gpsimd
        def _(g):
            for oc in range(FT):
                if oc >= 2:
                    g.wait_ge(fmm.h, 2 * (oc - 1))
                wl[oc % 2].inc(g.dma_start(out=wtb[oc % 2][:, :],
                                           in_=wt_in[oc * 128:(oc + 1) * 128, :]), 16)

        @blk.tensor
        def _(t):
            for oc in range(FT):
                t.wait_ge(wl[oc % 2].h, 16 * (oc // 2 + 1))
                for tc in range(NTC):
                    i = oc * NTC + tc
                    if i >= 2:
                        t.wait_ge(fev.h, i - 1)
                    for fc in range(FT):
                        t.matmul(ps[i % 2][0:128, 0:TC],
                                 wtb[oc % 2][:, fc * 128:(fc + 1) * 128],
                                 hb[fc][:, tc * TC:(tc + 1) * TC],
                                 start=(fc == 0), stop=False)
                    fmm.inc(t.matmul(ps[i % 2][0:128, 0:TC],
                                     bt_s[:, oc * 128:(oc + 1) * 128],
                                     tsc[:, tc * TC:(tc + 1) * TC],
                                     start=False, stop=True), 1)
                    if i >= 1:
                        pj = i - 1
                        t.wait_ge(fev.h, pj + 1)
                        t.matmul(ps[2 + (pj % 2)][0:1, 0:TC], onescb_s[:, :],
                                 f0st[pj % 2][:, :],
                                 start=(pj // 2 == 0), stop=(pj // 2 == FT - 1))
                        t.wait_ge(fsq.h, pj + 1)
                        fsm.inc(t.matmul(ps[4 + (pj % 2)][0:1, 0:TC], onescb_s[:, :],
                                         sq[pj % 2][:, :],
                                         start=(pj // 2 == 0), stop=(pj // 2 == FT - 1)), 1)
            pj = NI - 1
            t.wait_ge(fev.h, pj + 1)
            t.matmul(ps[2 + (pj % 2)][0:1, 0:TC], onescb_s[:, :],
                     f0st[pj % 2][:, :], start=False, stop=True)
            t.wait_ge(fsq.h, pj + 1)
            fsm.inc(t.matmul(ps[4 + (pj % 2)][0:1, 0:TC], onescb_s[:, :],
                             sq[pj % 2][:, :], start=False, stop=True), 1)

        @blk.vector
        def _(v):
            for oc in range(FT):
                for tc in range(NTC):
                    i = oc * NTC + tc
                    v.wait_ge(fmm.h, i + 1)
                    if i >= 2:
                        v.wait_ge(fsm.h, i - 1)
                        v.wait_ge(fst.h, 16 * (i - 1))
                    fev.inc(v.tensor_scalar_add(f0st[i % 2][:, :],
                                                ps[i % 2][0:128, 0:TC],
                                                bcol_s[:, oc:oc + 1]), 1)

        @blk.scalar
        def _(s):
            for i in range(NI):
                s.wait_ge(fev.h, i + 1)
                fsq.inc(s.activation(sq[i % 2][:, :], f0st[i % 2][:, :],
                                     AF.Square), 1)

        @blk.sync
        def _(sy):
            for oc in range(FT):
                for tc in range(NTC):
                    i = oc * NTC + tc
                    sy.wait_ge(fev.h, i + 1)
                    fst.inc(sy.dma_start(
                        out=f0_d[oc * 128:(oc + 1) * 128, tc * TC:(tc + 1) * TC],
                        in_=f0st[i % 2][:, :]), 16)

    if upto < 6:
        lb.done()
        return nc
    emit_rows(lb, [ps[2][0:1, 0:TC], ps[3][0:1, 0:TC], ps[4][0:1, 0:TC], ps[5][0:1, 0:TC]], rows, scr4, cst_s)
    emit_bc(lb, rows, rows_b, bc, onesrb_s, ps, lb.sem("cmBc6"))

    if upto < 7:
        lb.done()
        return nc
    xl = [lb.sem("xl0"), lb.sem("xl1")]
    f7l = [lb.sem("f7l0"), lb.sem("f7l1")]
    f7f = lb.sem("f7f")
    xf = lb.sem("xf")
    dv7 = lb.sem("dv7")
    sq7 = lb.sem("sq7")
    st7 = lb.sem("st7")
    mm7 = lb.sem("mm7")
    with lb.block() as blk:
        @blk.sync
        def _(sy):
            for oc in range(FT):
                if oc >= 2:
                    sy.wait_ge(f7f.h, oc - 1)
                f7l[oc % 2].inc(sy.dma_start(out=f0s[oc % 2][:, :],
                                             in_=f0_d[oc * 128:(oc + 1) * 128, :]), 16)
                for tc in range(NTC):
                    i = oc * NTC + tc
                    if i >= 2:
                        sy.wait_ge(xf.h, i - 1)
                    xl[i % 2].inc(sy.dma_start(
                        out=xst[i % 2][:, :],
                        in_=x1_in[oc * 128:(oc + 1) * 128, tc * TC:(tc + 1) * TC]), 16)

        @blk.vector
        def _(v):
            for oc in range(FT):
                v.wait_ge(f7l[oc % 2].h, 16 * (oc // 2 + 1))
                for tc in range(NTC):
                    i = oc * NTC + tc
                    sl = slice(tc * TC, (tc + 1) * TC)
                    v.wait_ge(xl[i % 2].h, 16 * (i // 2 + 1))
                    if i >= 2:
                        v.wait_ge(st7.h, 16 * (i - 1))
                        v.wait_ge(mm7.h, (i - 1))
                    tmul = v.tensor_tensor(t1[:, :], f0s[oc % 2][:, sl], bc[:, sl],
                                           op=ALU.mult)
                    if tc == NTC - 1:
                        f7f.inc(tmul, 1)
                    v.tensor_tensor(t2[:, :], t1[:, :],
                                    bc[:, T + tc * TC:T + (tc + 1) * TC],
                                    op=ALU.subtract)
                    xf.inc(v.tensor_tensor(x2t[i % 2][:, :], xst[i % 2][:, :],
                                           t2[:, :], op=ALU.add), 1)
                    dv7.inc(v.tensor_copy(xs2[i % 2][:, tc * TC:(tc + 1) * TC],
                                          x2t[i % 2][:, :]), 1)

        @blk.scalar
        def _(s):
            for oc in range(FT):
                for tc in range(NTC):
                    i = oc * NTC + tc
                    s.wait_ge(dv7.h, i + 1)
                    if i >= 2:
                        s.wait_ge(mm7.h, i - 1)
                    sq7.inc(s.activation(sq[i % 2][:, :],
                                         xs2[i % 2][:, tc * TC:(tc + 1) * TC],
                                         AF.Square), 1)

        @blk.tensor
        def _(t):
            for oc in range(FT):
                for tc in range(NTC):
                    i = oc * NTC + tc
                    t.wait_ge(dv7.h, i + 1)
                    t.matmul(ps[2 + tc][0:1, 0:TC], onescb_s[:, :],
                             xs2[i % 2][:, tc * TC:(tc + 1) * TC],
                             start=(oc == 0), stop=(oc == FT - 1))
                    t.wait_ge(sq7.h, i + 1)
                    mm7.inc(t.matmul(ps[4 + tc][0:1, 0:TC], onescb_s[:, :],
                                     sq[i % 2][:, :],
                                     start=(oc == 0), stop=(oc == FT - 1)), 1)

        @blk.gpsimd
        def _(g):
            for oc in range(FT):
                for tc in range(NTC):
                    i = oc * NTC + tc
                    g.wait_ge(xf.h, i + 1)
                    st7.inc(g.dma_start(
                        out=o_x2[oc * 128:(oc + 1) * 128, tc * TC:(tc + 1) * TC],
                        in_=x2t[i % 2][:, :]), 16)

    emit_rows(lb, [ps[2][0:1, 0:TC], ps[3][0:1, 0:TC], ps[4][0:1, 0:TC], ps[5][0:1, 0:TC]], rows, scr4, cst_s)
    emit_bc(lb, rows, rows_b, bc, onesrb_s, ps, lb.sem("cmBc7"))

    vwb = [lb.sbx(f"vwb{j}", 128, 32 * 256, BF16) for j in range(2)]

    if upto < 8:
        lb.done()
        return nc
    c8l = [lb.sem("c8l0"), lb.sem("c8l1")]
    c8f = lb.sem("c8f")
    with lb.block() as blk:
        @blk.gpsimd
        def _(g):
            for i in range(FT):
                if i >= 2:
                    g.wait_ge(c8f.h, i - 1)
                c8l[i % 2].inc(g.dma_start(out=xs2[i % 2][:, :],
                                           in_=o_x2[i * 128:(i + 1) * 128, :]), 16)

        @blk.vector
        def _(v):
            for i in range(FT):
                v.wait_ge(c8l[i % 2].h, 16 * (i // 2 + 1))
                for tc in range(NTC):
                    sl = slice(tc * TC, (tc + 1) * TC)
                    v.tensor_tensor(hb[i][:, sl], xs2[i % 2][:, sl],
                                    bc[:, tc * TC:(tc + 1) * TC], op=ALU.mult)
                inst = None
                for tc in range(NTC):
                    sl = slice(tc * TC, (tc + 1) * TC)
                    inst = v.tensor_tensor(hb[i][:, sl], hb[i][:, sl],
                                           bc[:, T + tc * TC:T + (tc + 1) * TC],
                                           op=ALU.subtract)
                c8f.inc(inst, 1)

    if upto < 9:
        lb.done()
        return nc
    ql = [lb.sem("ql0"), lb.sem("ql1")]
    qmm = lb.sem("qmm")
    qev = lb.sem("qev")
    qst = lb.sem("qst")
    with lb.block() as blk:
        @blk.gpsimd
        def _(g):
            for qc in range(64):
                if qc >= 2:
                    g.wait_ge(qmm.h, (qc - 1) * NTC)
                ql[qc % 2].inc(g.dma_start(out=wtb[qc % 2][:, :],
                                           in_=qk_in[qc * 128:(qc + 1) * 128, :]), 16)

        @blk.tensor
        def _(t):
            for qc in range(64):
                t.wait_ge(ql[qc % 2].h, 16 * (qc // 2 + 1))
                for tc in range(NTC):
                    i = qc * NTC + tc
                    if i >= 2:
                        t.wait_ge(qev.h, i - 1)
                    inst = None
                    for fc in range(FT):
                        inst = t.matmul(ps[i % 2][0:128, 0:TC],
                                        wtb[qc % 2][:, fc * 128:(fc + 1) * 128],
                                        hb[fc][:, tc * TC:(tc + 1) * TC],
                                        start=(fc == 0), stop=(fc == FT - 1))
                    qmm.inc(inst, 1)

        @blk.vector
        def _(v):
            for qc in range(64):
                for tc in range(NTC):
                    i = qc * NTC + tc
                    v.wait_ge(qmm.h, i + 1)
                    if i >= 2:
                        v.wait_ge(qst.h, 16 * (i - 1))
                    qev.inc(v.tensor_scalar_add(qko[i % 2][:, :],
                                                ps[i % 2][0:128, 0:TC],
                                                qkbc_s[:, qc:qc + 1]), 1)

        @blk.sync
        def _(sy):
            for qc in range(64):
                dst = o_qf if qc < 32 else o_kf
                r0 = (qc % 32) * 128
                for tc in range(NTC):
                    i = qc * NTC + tc
                    sy.wait_ge(qev.h, i + 1)
                    qst.inc(sy.dma_start(
                        out=dst[r0:r0 + 128, tc * TC:(tc + 1) * TC],
                        in_=qko[i % 2][:, :]), 16)

    if upto < 10:
        lb.done()
        return nc
    vl = [lb.sem("vl0"), lb.sem("vl1")]
    vmm = lb.sem("vmm")
    vev = lb.sem("vev")
    vst = lb.sem("vst")
    with lb.block() as blk:
        @blk.gpsimd
        def _(g):
            for vc in range(16):
                if vc >= 2:
                    g.wait_ge(vmm.h, (vc - 1) * 8)
                vl[vc % 2].inc(g.dma_start(out=vwb[vc % 2][:, :],
                                           in_=vw_in[vc * 128:(vc + 1) * 128, :]), 16)

        @blk.tensor
        def _(t):
            for vc in range(16):
                t.wait_ge(vl[vc % 2].h, 16 * (vc // 2 + 1))
                for tt in range(8):
                    i = vc * 8 + tt
                    if i >= 2:
                        t.wait_ge(vev.h, i - 1)
                    inst = None
                    for fc in range(FT):
                        inst = t.matmul(ps[i % 2][0:128, 0:256],
                                        hb[fc][:, tt * 128:(tt + 1) * 128],
                                        vwb[vc % 2][:, fc * 256:(fc + 1) * 256],
                                        start=(fc == 0), stop=(fc == FT - 1))
                    vmm.inc(inst, 1)

        @blk.vector
        def _(v):
            for vc in range(16):
                for tt in range(8):
                    i = vc * 8 + tt
                    v.wait_ge(vmm.h, i + 1)
                    if i >= 2:
                        v.wait_ge(vst.h, 16 * (i - 1))
                    vev.inc(v.scalar_tensor_tensor(
                        vto[i % 2][:, :], ps[i % 2][0:128, 0:256], 1.0,
                        vbbc_s[:, vc * 256:(vc + 1) * 256],
                        op0=ALU.mult, op1=ALU.add), 1)

        @blk.sync
        def _(sy):
            for vc in range(16):
                for tt in range(8):
                    i = vc * 8 + tt
                    sy.wait_ge(vev.h, i + 1)
                    vst.inc(sy.dma_start(
                        out=o_vt[tt * 128:(tt + 1) * 128, vc * 256:(vc + 1) * 256],
                        in_=vto[i % 2][:, :]), 16)

    with lb.block() as blk:
        @blk.sync
        def _(sy):
            sy.wait_ge(st7.h, st7.n)
            sy.wait_ge(qst.h, qst.n)
            sy.wait_ge(vst.h, vst.n)
            sy.wait_ge(st.h, st.n)

    lb.done()
    return nc



def build_launch_D():
    lb = LB()
    nc = lb.nc
    x2_in = lb.inp("x2", (D, T))
    qf_in = lb.inp("qf", (D, T), BF16)
    kfull_in = lb.inp("kfull", (D, 4 * T), BF16)
    vblk_in = lb.inp("vblk", (D, 4 * T), BF16)
    ow_in = lb.inp("owb", (D, D))
    w1f_in = lb.inp("w1f", (3 * D, D))
    ob_in = lb.inp("obc", (128, 32))
    vm_in = lb.inp("vmr", (1, T))
    tm_in = lb.inp("tmr", (1, T))
    m0_in = lb.inp("m0r", (1, T))
    onescb_in = lb.inp("onescb", (128, 1), BF16)
    onesrb_in = lb.inp("onesrb", (1, 128), BF16)
    onesrf_in = lb.inp("onesrf", (1, 128))
    onescf_in = lb.inp("onescf", (128, 1))
    cst_in = lb.inp("cst", (128, 8))

    o_x3 = lb.out("x3", (D, T))
    o_ln3 = lb.out("ln3", (1, 3 * T))
    o_mod = lb.out("modv", (1, 2))
    o_h30 = lb.out("h30", (128, 32))
    o_s0 = lb.out("s0", (1, 1))
    o_w1b = lb.out("w1bf", (3 * D, D), BF16)

    ao_d = nc.dram_tensor("aod", [D, T], BF16, kind="Internal").ap()

    qb = [lb.sb(f"qb{i}", 128, T, BF16) for i in range(FT)]
    kb = [lb.sb(f"kb{j}", 128, 4 * T, BF16) for j in range(2)]
    vb = [lb.sb(f"vb{j}", 128, 4 * T, BF16) for j in range(2)]
    expb = [lb.sb(f"expb{j}", 128, 2 * TC, BF16) for j in range(2)]
    rsr = lb.sb("rsr", 1, 4)
    rs = [lb.sb(f"rs{j}", 1, TC) for j in range(2)]
    rbc = [lb.sb(f"rbc{j}", 128, TC, BF16) for j in range(2)]
    owb = [lb.sb(f"owb{j}", 128, D, BF16) for j in range(2)]
    ob_s = lb.sb("ob_s", 128, 32)
    aost = [lb.sb(f"aost{j}", 128, TC, BF16) for j in range(2)]
    aos = [lb.sb(f"aos{j}", 128, T, BF16) for j in range(2)]
    sq = [lb.sb(f"sq{j}", 128, TC, BF16) for j in range(2)]
    xst = [lb.sb(f"xst{j}", 128, TC) for j in range(2)]
    t1 = lb.sb("t1", 128, TC, BF16)
    t2 = lb.sb("t2", 128, TC, BF16)
    x3t = [lb.sb(f"x3t{j}", 128, TC) for j in range(2)]
    x3b = [lb.sb(f"x3b{j}", 128, T, BF16) for j in range(2)]
    rows = lb.sb("rows", 1, 3 * T)
    rows_b = lb.sb("rows_b", 1, 2 * T, BF16)
    scr4 = lb.sb("scr4", 1, 3 * T + 16)
    bc = lb.sb("bc", 128, 2 * T, BF16)
    modbc = lb.sb("modbc", 128, 4 * T, BF16)
    tmp1 = lb.sb("tmp1", 128, T, BF16)
    tmp2 = lb.sb("tmp2", 128, T, BF16)
    junk = lb.sb("junk", 128, T, BF16)
    mvc = lb.sb("mvc", 128, 32)
    tvc = lb.sb("tvc", 128, 32)
    h30c = lb.sb("h30c", 128, 32)
    colsc = lb.sb("colsc", 128, 4)
    vm_s = lb.sb("vm_s", 1, T)
    tm_s = lb.sb("tm_s", 1, T)
    m0_s = lb.sb("m0_s", 1, T)
    onescb_s = lb.sb("onescb_s", 128, 1, BF16)
    onesrb_s = lb.sb("onesrb_s", 1, 128, BF16)
    onesrf_s = lb.sb("onesrf_s", 1, 128)
    onescf_s = lb.sb("onescf_s", 128, 1)
    cst_s = lb.sb("cst_s", 128, 8)

    ps = [lb.ps(f"ps{i}") for i in range(4)]

    ld = lb.sem("ld")
    st = lb.sem("st")

    with lb.block() as blk:
        @blk.gpsimd
        def _(g):
            for i in range(FT):
                ld.inc(g.dma_start(out=qb[i][:, :], in_=qf_in[i * 128:(i + 1) * 128, :]), 16)

        @blk.sync
        def _(sy):
            for dst, srcin in ((ob_s[:, :], ob_in), (vm_s[:, :], vm_in),
                               (tm_s[:, :], tm_in), (m0_s[:, :], m0_in),
                               (onesrf_s[:, :], onesrf_in), (onescf_s[:, :], onescf_in),
                               (cst_s[:, :], cst_in)):
                ld.inc(sy.dma_start(out=dst, in_=srcin), 16)

        @blk.scalar
        def _(s):
            ld.inc(s.dma_start(out=onescb_s[:, :], in_=onescb_in), 16)
            ld.inc(s.dma_start(out=onesrb_s[:, :], in_=onesrb_in), 16)
        ld_all = ld.n

        @blk.vector
        def _(v):
            v.wait_ge(ld.h, ld_all)

    kvl = [lb.sem("kvl0"), lb.sem("kvl1")]
    smm = lb.sem("smm")
    sex = lb.sem("sex")
    ctxd = lb.sem("ctxd")
    d1 = lb.sem("d1")
    d2 = lb.sem("d2")
    d3 = lb.sem("d3")
    cev = lb.sem("cev")
    ps_sc = [lb.ps(f"ps_sc{j}", 128, 2 * TC) for j in range(2)]
    seqD = [(h, kt) for h in range(32) for kt in range(FT)]
    with lb.block() as blk:
        @blk.sync
        def _(sy):
            for h in range(32):
                if h >= 2:
                    sy.wait_ge(smm.h, (h - 1) * FT)
                    sy.wait_ge(ctxd.h, (h - 1) * FT)
                kvl[h % 2].inc(sy.dma_start(out=kb[h % 2][:, :],
                                            in_=kfull_in[h * 128:(h + 1) * 128, :]), 16)
                kvl[h % 2].inc(sy.dma_start(out=vb[h % 2][:, :],
                                            in_=vblk_in[h * 128:(h + 1) * 128, :]), 16)

        @blk.gpsimd
        def _(g):
            for h in range(32):
                if h >= 2:
                    g.wait_ge(smm.h, (h - 1) * FT)
                for j in range(3):
                    k = h * 3 + j
                    st.inc(g.dma_start(
                        out=o_w1b[k * 128:(k + 1) * 128, :],
                        in_=w1f_in[k * 128:(k + 1) * 128, :]), 16)

        @blk.tensor
        def _(t):
            for idx in range(len(seqD) + 1):
                if idx < len(seqD):
                    h, kt = seqD[idx]
                    if kt == 0:
                        t.wait_ge(kvl[h % 2].h, 32 * (h // 2 + 1))
                    if idx >= 2:
                        t.wait_ge(sex.h, idx - 1)
                    t.matmul(ps_sc[idx % 2][0:128, 0:TC],
                             kb[h % 2][:, kt * 128:(kt + 1) * 128],
                             qb[h][:, 0:TC], start=True, stop=True)
                    smm.inc(t.matmul(ps_sc[idx % 2][0:128, TC:2 * TC],
                                     kb[h % 2][:, kt * 128:(kt + 1) * 128],
                                     qb[h][:, TC:2 * TC], start=True, stop=True), 1)
                if idx >= 1:
                    h, kt = seqD[idx - 1]
                    if kt == 0 and h >= 1:
                        t.wait_ge(cev.h, 2 * h)
                    t.wait_ge(sex.h, idx)
                    eb = expb[(idx - 1) % 2]
                    t.matmul(ps[0][0:1, 0:TC], onescb_s[:, :], eb[:, 0:TC],
                             start=(kt == 0), stop=(kt == FT - 1),
                             tile_position=(0, 0))
                    t.matmul(ps[1][32:33, 0:TC], onescb_s[:, :], eb[:, TC:2 * TC],
                             start=(kt == 0), stop=(kt == FT - 1),
                             tile_position=(0, 32))
                    t.matmul(ps[2][0:128, 0:TC],
                             vb[h % 2][:, kt * 128:(kt + 1) * 128], eb[:, 0:TC],
                             start=(kt == 0), stop=(kt == FT - 1))
                    ctxd.inc(t.matmul(ps[3][0:128, 0:TC],
                                      vb[h % 2][:, kt * 128:(kt + 1) * 128],
                                      eb[:, TC:2 * TC],
                                      start=(kt == 0), stop=(kt == FT - 1)), 1)
                    if kt == FT - 1:
                        t.wait_ge(d1.h, 2 * h + 2)
                        if h >= 1:
                            t.wait_ge(d3.h, 2 * h)
                        d2.inc(t.matmul(ps[0][0:128, 0:TC], onesrf_s[:, :],
                                        rs[0][0:1, 0:TC], start=True, stop=True), 1)
                        d2.inc(t.matmul(ps[1][0:128, 0:TC], onesrf_s[:, :],
                                        rs[1][0:1, 0:TC], start=True, stop=True), 1)

        @blk.scalar
        def _(s):
            for idx in range(len(seqD)):
                h, kt = seqD[idx]
                s.wait_ge(smm.h, idx + 1)
                if idx >= 2:
                    s.wait_ge(ctxd.h, idx - 1)
                sex.inc(s.activation(expb[idx % 2][:, :],
                                     ps_sc[idx % 2][0:128, 0:2 * TC],
                                     AF.Exp, scale=L1_SCALE, bias=cst_s[:, 2:3]), 1)
                if kt == FT - 1:
                    s.wait_ge(d2.h, 2 * h + 2)
                    if h >= 1:
                        s.wait_ge(cev.h, 2 * h)
                    d3.inc(s.activation(rbc[0][:, :], ps[0][0:128, 0:TC], AF.Copy), 1)
                    d3.inc(s.activation(rbc[1][:, :], ps[1][0:128, 0:TC], AF.Copy), 1)

        @blk.vector
        def _(v):
            for h in range(32):
                v.wait_ge(ctxd.h, (h + 1) * FT)
                if h >= 1:
                    v.wait_ge(d2.h, 2 * h)
                d1.inc(v.reciprocal(rs[0][0:1, 0:TC], ps[0][0:1, 0:TC]), 1)
                d1.inc(v.reciprocal(rs[1][0:1, 0:TC], ps[1][32:33, 0:TC]), 1)
                v.wait_ge(d3.h, 2 * h + 2)
                cev.inc(v.scalar_tensor_tensor(qb[h][:, 0:TC], ps[2][0:128, 0:TC],
                                               1.0, rbc[0][:, :],
                                               op0=ALU.mult, op1=ALU.mult), 1)
                cev.inc(v.scalar_tensor_tensor(qb[h][:, TC:2 * TC],
                                               ps[3][0:128, 0:TC], 1.0, rbc[1][:, :],
                                               op0=ALU.mult, op1=ALU.mult), 1)

    wl = [lb.sem("wl0"), lb.sem("wl1")]
    fmm = lb.sem("fmm")
    fev = lb.sem("fev")
    fsq = lb.sem("fsq")
    fsm = lb.sem("fsm")
    fst = lb.sem("fst")
    NI = FT * NTC
    with lb.block() as blk:
        @blk.gpsimd
        def _(g):
            for oc in range(FT):
                if oc >= 2:
                    g.wait_ge(fmm.h, 2 * (oc - 1))
                wl[oc % 2].inc(g.dma_start(out=owb[oc % 2][:, :],
                                           in_=ow_in[oc * 128:(oc + 1) * 128, :]), 16)

        @blk.tensor
        def _(t):
            for oc in range(FT):
                t.wait_ge(wl[oc % 2].h, 16 * (oc // 2 + 1))
                for tc in range(NTC):
                    i = oc * NTC + tc
                    if i >= 2:
                        t.wait_ge(fev.h, i - 1)
                    inst = None
                    for h in range(FT):
                        inst = t.matmul(ps[i % 2][0:128, 0:TC],
                                        owb[oc % 2][:, h * 128:(h + 1) * 128],
                                        qb[h][:, tc * TC:(tc + 1) * TC],
                                        start=(h == 0), stop=(h == FT - 1))
                    fmm.inc(inst, 1)
                    if i >= 1:
                        pj = i - 1
                        t.wait_ge(fev.h, pj + 1)
                        t.matmul(ps[2 + (pj % 2)][0:1, 0:TC], onescb_s[:, :],
                                 aost[pj % 2][:, :],
                                 start=(pj // 2 == 0), stop=(pj // 2 == FT - 1))
                        t.wait_ge(fsq.h, pj + 1)
                        fsm.inc(t.matmul(ps_sc[pj % 2][0:1, 0:TC], onescb_s[:, :],
                                         sq[pj % 2][:, :],
                                         start=(pj // 2 == 0), stop=(pj // 2 == FT - 1)), 1)
            pj = NI - 1
            t.wait_ge(fev.h, pj + 1)
            t.matmul(ps[2 + (pj % 2)][0:1, 0:TC], onescb_s[:, :],
                     aost[pj % 2][:, :], start=False, stop=True)
            t.wait_ge(fsq.h, pj + 1)
            fsm.inc(t.matmul(ps_sc[pj % 2][0:1, 0:TC], onescb_s[:, :],
                             sq[pj % 2][:, :], start=False, stop=True), 1)

        @blk.vector
        def _(v):
            for oc in range(FT):
                for tc in range(NTC):
                    i = oc * NTC + tc
                    v.wait_ge(fmm.h, i + 1)
                    if i >= 2:
                        v.wait_ge(fsm.h, i - 1)
                        v.wait_ge(fst.h, 16 * (i - 1))
                    fev.inc(v.tensor_scalar_add(aost[i % 2][:, :],
                                                ps[i % 2][0:128, 0:TC],
                                                ob_s[:, oc:oc + 1]), 1)

        @blk.scalar
        def _(s):
            for i in range(NI):
                s.wait_ge(fev.h, i + 1)
                fsq.inc(s.activation(sq[i % 2][:, :], aost[i % 2][:, :],
                                     AF.Square), 1)

        @blk.sync
        def _(sy):
            for oc in range(FT):
                for tc in range(NTC):
                    i = oc * NTC + tc
                    sy.wait_ge(fev.h, i + 1)
                    fst.inc(sy.dma_start(
                        out=ao_d[oc * 128:(oc + 1) * 128, tc * TC:(tc + 1) * TC],
                        in_=aost[i % 2][:, :]), 16)

    emit_rows(lb, [ps[2][0:1, 0:TC], ps[3][0:1, 0:TC], ps_sc[0][0:1, 0:TC], ps_sc[1][0:1, 0:TC]], rows, scr4, cst_s)
    emit_bc(lb, rows, rows_b, bc, onesrb_s, ps, lb.sem("cmBd4"))

    xl = [lb.sem("xl0"), lb.sem("xl1")]
    f5l = [lb.sem("f5l0"), lb.sem("f5l1")]
    f5f = lb.sem("f5f")
    xf = lb.sem("xf")
    dv5 = lb.sem("dv5")
    sq5 = lb.sem("sq5")
    st5 = lb.sem("st5")
    mm5 = lb.sem("mm5")
    with lb.block() as blk:
        @blk.sync
        def _(sy):
            for oc in range(FT):
                if oc >= 2:
                    sy.wait_ge(f5f.h, oc - 1)
                f5l[oc % 2].inc(sy.dma_start(out=aos[oc % 2][:, :],
                                             in_=ao_d[oc * 128:(oc + 1) * 128, :]), 16)
                for tc in range(NTC):
                    i = oc * NTC + tc
                    if i >= 2:
                        sy.wait_ge(xf.h, i - 1)
                    xl[i % 2].inc(sy.dma_start(
                        out=xst[i % 2][:, :],
                        in_=x2_in[oc * 128:(oc + 1) * 128, tc * TC:(tc + 1) * TC]), 16)

        @blk.vector
        def _(v):
            for oc in range(FT):
                v.wait_ge(f5l[oc % 2].h, 16 * (oc // 2 + 1))
                for tc in range(NTC):
                    i = oc * NTC + tc
                    sl = slice(tc * TC, (tc + 1) * TC)
                    v.wait_ge(xl[i % 2].h, 16 * (i // 2 + 1))
                    if i >= 2:
                        v.wait_ge(st5.h, 16 * (i - 1))
                        v.wait_ge(mm5.h, (i - 1))
                    tmul = v.tensor_tensor(t1[:, :], aos[oc % 2][:, sl], bc[:, sl],
                                           op=ALU.mult)
                    if tc == NTC - 1:
                        f5f.inc(tmul, 1)
                    v.tensor_tensor(t2[:, :], t1[:, :],
                                    bc[:, T + tc * TC:T + (tc + 1) * TC],
                                    op=ALU.subtract)
                    xf.inc(v.tensor_tensor(x3t[i % 2][:, :], xst[i % 2][:, :],
                                           t2[:, :], op=ALU.add), 1)
                    dv5.inc(v.tensor_copy(x3b[i % 2][:, tc * TC:(tc + 1) * TC],
                                          x3t[i % 2][:, :]), 1)

        @blk.scalar
        def _(s):
            for oc in range(FT):
                for tc in range(NTC):
                    i = oc * NTC + tc
                    s.wait_ge(dv5.h, i + 1)
                    if i >= 2:
                        s.wait_ge(mm5.h, i - 1)
                    sq5.inc(s.activation(sq[i % 2][:, :],
                                         x3b[i % 2][:, tc * TC:(tc + 1) * TC],
                                         AF.Square), 1)

        @blk.tensor
        def _(t):
            for oc in range(FT):
                for tc in range(NTC):
                    i = oc * NTC + tc
                    t.wait_ge(dv5.h, i + 1)
                    t.matmul(ps[2 + tc][0:1, 0:TC], onescb_s[:, :],
                             x3b[i % 2][:, tc * TC:(tc + 1) * TC],
                             start=(oc == 0), stop=(oc == FT - 1))
                    t.wait_ge(sq5.h, i + 1)
                    mm5.inc(t.matmul(ps_sc[tc][0:1, 0:TC], onescb_s[:, :],
                                     sq[i % 2][:, :],
                                     start=(oc == 0), stop=(oc == FT - 1)), 1)

        @blk.gpsimd
        def _(g):
            for oc in range(FT):
                for tc in range(NTC):
                    i = oc * NTC + tc
                    g.wait_ge(xf.h, i + 1)
                    st5.inc(g.dma_start(
                        out=o_x3[oc * 128:(oc + 1) * 128, tc * TC:(tc + 1) * TC],
                        in_=x3t[i % 2][:, :]), 16)

    emit_rows(lb, [ps[2][0:1, 0:TC], ps[3][0:1, 0:TC], ps_sc[0][0:1, 0:TC], ps_sc[1][0:1, 0:TC]], rows, scr4, cst_s)

    with lb.block() as blk:
        @blk.vector
        def _(v):
            v.tensor_tensor(R_(scr4, 0), vm_s[0:1, 0:T], R_(rows, 0), op=ALU.mult)
            v.tensor_tensor(R_(scr4, 1), tm_s[0:1, 0:T], R_(rows, 0), op=ALU.mult)
            v.tensor_tensor(R_(scr4, 2), m0_s[0:1, 0:T], R_(rows, 0), op=ALU.mult)
    with lb.block() as blk:
        @blk.vector
        def _(v):
            v.tensor_tensor(scr4[0:1, 3 * T + 8:3 * T + 8 + 1].rearrange("a b -> a b")
                            if False else junk[0:1, 0:T],
                            m0_s[0:1, 0:T], R_(rows, 1), op=ALU.mult)
    with lb.block() as blk:
        @blk.vector
        def _(v):
            v.tensor_reduce(scr4[0:1, 3 * T:3 * T + 1], junk[0:1, 0:T].rearrange("a b -> a b"),
                            axis=AXX, op=ALU.add)
            v.tensor_copy(R_(rows_b, 0), R_(rows, 2))
            v.tensor_copy(R_(rows_b, 1), R_(scr4, 0))
    with lb.block() as blk:
        @blk.vector
        def _(v):
            pass
    bcm = lb.sem("bcm")
    with lb.block() as blk:
        @blk.tensor
        def _(t):
            for r in range(2):
                for tc in range(NTC):
                    i = r * NTC + tc
                    bcm.inc(t.matmul(ps[i][0:128, 0:TC], onesrb_s[:, :],
                                     R_(rows_b, r)[:, tc * TC:(tc + 1) * TC],
                                     start=True, stop=True), 1)

        @blk.scalar
        def _(s):
            for r in range(2):
                for tc in range(NTC):
                    i = r * NTC + tc
                    s.wait_ge(bcm.h, i + 1)
                    s.activation(modbc[:, r * T + tc * TC:r * T + (tc + 1) * TC],
                                 ps[i][0:128, 0:TC], AF.Copy)
    with lb.block() as blk:
        @blk.vector
        def _(v):
            v.tensor_copy(R_(rows_b, 0), R_(scr4, 1))
            v.tensor_copy(R_(rows_b, 1), R_(scr4, 2))
    bcm2 = lb.sem("bcm2")
    with lb.block() as blk:
        @blk.tensor
        def _(t):
            for r in range(2):
                for tc in range(NTC):
                    i = r * NTC + tc
                    bcm2.inc(t.matmul(ps[i][0:128, 0:TC], onesrb_s[:, :],
                                      R_(rows_b, r)[:, tc * TC:(tc + 1) * TC],
                                      start=True, stop=True), 1)

        @blk.scalar
        def _(s):
            for r in range(2):
                for tc in range(NTC):
                    i = r * NTC + tc
                    s.wait_ge(bcm2.h, i + 1)
                    s.activation(modbc[:, (2 + r) * T + tc * TC:(2 + r) * T + (tc + 1) * TC],
                                 ps[i][0:128, 0:TC], AF.Copy)
    m6l = [lb.sem("m6l0"), lb.sem("m6l1")]
    m6a = lb.sem("m6a")
    m6b = lb.sem("m6b")
    m6f = lb.sem("m6f")
    with lb.block() as blk:
        @blk.gpsimd
        def _(g):
            for i in range(FT):
                if i >= 2:
                    g.wait_ge(m6f.h, i - 1)
                m6l[i % 2].inc(g.dma_start(out=x3b[i % 2][:, :],
                                           in_=o_x3[i * 128:(i + 1) * 128, :]), 16)

        @blk.vector
        def _(v):
            for i in range(FT):
                v.wait_ge(m6l[i % 2].h, 16 * (i // 2 + 1))
                m6a.inc(v.tensor_tensor(tmp1[:, :], x3b[i % 2][:, :],
                                        modbc[:, 0:T], op=ALU.subtract), 1)
                v.wait_ge(m6b.h, i + 1)
                v.scalar_tensor_tensor(junk[:, :], tmp2[:, :], 1.0,
                                       modbc[:, T:2 * T], op0=ALU.mult,
                                       op1=ALU.mult, accum_out=mvc[:, i:i + 1])
                v.scalar_tensor_tensor(junk[:, :], tmp2[:, :], 1.0,
                                       modbc[:, 2 * T:3 * T], op0=ALU.mult,
                                       op1=ALU.mult, accum_out=tvc[:, i:i + 1])
                m6f.inc(v.scalar_tensor_tensor(junk[:, :], x3b[i % 2][:, :], 1.0,
                                               modbc[:, 3 * T:4 * T], op0=ALU.mult,
                                               op1=ALU.mult,
                                               accum_out=h30c[:, i:i + 1]), 1)

        @blk.scalar
        def _(s):
            for i in range(FT):
                s.wait_ge(m6a.h, i + 1)
                m6b.inc(s.activation(tmp2[:, :], tmp1[:, :], AF.Abs), 1)

    d7 = lb.sem("d7")
    with lb.block() as blk:
        @blk.vector
        def _(v):
            v.tensor_reduce(colsc[:, 0:1], mvc[:, :], axis=AXX, op=ALU.add)
            v.tensor_reduce(colsc[:, 1:2], tvc[:, :], axis=AXX, op=ALU.add)
    with lb.block() as blk:
        @blk.tensor
        def _(t):
            d7.inc(t.matmul(ps[0][0:1, 0:1], colsc[:, 0:1], onescf_s[:, :],
                            start=True, stop=True), 1)
            d7.inc(t.matmul(ps[1][0:1, 0:1], colsc[:, 1:2], onescf_s[:, :],
                            start=True, stop=True), 1)

        @blk.scalar
        def _(s):
            s.wait_ge(d7.h, 2)
            s.activation(scr4[0:1, 3 * T + 2:3 * T + 3], ps[0][0:1, 0:1], AF.Copy)
            s.activation(scr4[0:1, 3 * T + 3:3 * T + 4], ps[1][0:1, 0:1], AF.Copy)

    with lb.block() as blk:
        @blk.sync
        def _(sy):
            st.inc(sy.dma_start(out=o_ln3, in_=rows[0:1, 0:3 * T]), 16)
            st.inc(sy.dma_start(out=o_mod, in_=scr4[0:1, 3 * T + 2:3 * T + 4]), 16)
            st.inc(sy.dma_start(out=o_h30, in_=h30c[:, :]), 16)
            st.inc(sy.dma_start(out=o_s0, in_=scr4[0:1, 3 * T:3 * T + 1]), 16)
            sy.wait_ge(st5.h, st5.n)
            sy.wait_ge(st.h, st.n)

    lb.done()
    return nc



EFUNC = [AF.Tanh, AF.Gelu, AF.Silu]


def build_launch_E():
    lb = LB()
    nc = lb.nc
    x3_in = lb.inp("x3", (D, T))
    ln3_in = lb.inp("ln3", (1, 3 * T))
    mod_in = lb.inp("modall", (1, 8))
    h30_in = lb.inp("h30all", (128, 128))
    s0_in = lb.inp("s0all", (1, 4))
    gwm_in = lb.inp("gwm", (128, 32 * 3))
    gwl_in = lb.inp("gwl", (1, 3))
    gb_in = lb.inp("gb", (1, 3))
    w1_in = lb.inp("w1b", (3 * D, D), BF16)
    w2_in = lb.inp("w2b", (3 * D, D), BF16)
    b1_in = lb.inp("b1c", (128, 96))
    b2_in = lb.inp("b2n", (3, D))
    onescb_in = lb.inp("onescb", (128, 1), BF16)
    onesrb_in = lb.inp("onesrb", (1, 128), BF16)
    onesrf_in = lb.inp("onesrf", (1, 128))
    cst_in = lb.inp("cst", (128, 8))

    o_out = lb.out("out", (D, T))

    moe_d = nc.dram_tensor("moed", [D, T], BF16, kind="Internal").ap()
    ab_d = nc.dram_tensor("abd", [1, 8], F32, kind="Internal").ap()
    tv_d = nc.dram_tensor("tvd", [2, T], F32, kind="Internal").ap()

    h3tc = [lb.sb(f"h3tc{i}", 128, TC, BF16) for i in range(FT)]
    rows = lb.sb("rows", 1, 3 * T)
    rows_b = lb.sb("rows_b", 1, 2 * T, BF16)
    scr4 = lb.sb("scr4", 1, 3 * T + 16)
    bc = lb.sb("bc", 128, 2 * T, BF16)
    cbc = lb.sb("cbc", 128, 3 * T, BF16)
    ceT = lb.sb("ceT", 3, T, BF16)
    gwm_b = lb.sb("gwm_b", 128, 96, BF16)
    gwm_f = lb.sb("gwm_f", 128, 96)
    gwl_s = lb.sb("gwl_s", 1, 3)
    gb_s = lb.sb("gb_s", 1, 3)
    b1_s = lb.sb("b1_s", 128, 96)
    b2_s = lb.sb("b2_s", 3, D, BF16)
    mod_s = lb.sb("mod_s", 1, 8)
    h30a = lb.sb("h30a", 128, 128)
    h30v = lb.sb("h30v", 128, 32)
    s0_s = lb.sb("s0_s", 1, 4)
    s0col = lb.sb("s0col", 128, 1)
    lrow = lb.sb("lrow", 1, 32)
    rowb = lb.sb("rowb", 1, 3)
    rowbc = lb.sb("rowbc", 128, 3)
    acol = lb.sb("acol", 128, 6)
    gl_sb = lb.sb("gl_sb", 128, 24)
    colb = lb.sb("colb", 128, 10)
    xstc = [lb.sb(f"xstc{j}", 128, TC, BF16) for j in range(2)]
    w1b = [lb.sb(f"w1b{j}", 128, D, BF16) for j in range(2)]
    w2b = w1b
    moest = [lb.sb(f"moest{j}", 128, TC, BF16) for j in range(2)]
    sq = [lb.sb(f"sqe{j}", 128, TC, BF16) for j in range(2)]
    onescb_s = lb.sb("onescb_s", 128, 1, BF16)
    onesrb_s = lb.sb("onesrb_s", 1, 128, BF16)
    onesrf_s = lb.sb("onesrf_s", 1, 128)
    cst_s = lb.sb("cst_s", 128, 8)
    negm = lb.sb("negm", 128, 8)
    sumc = lb.sb("sumc", 128, 8)
    rsc = lb.sb("rsc", 128, 8)
    tv0c = lb.sb("tv0c", 128, 8)
    mnc = lb.sb("mnc", 128, 8)
    s3c = lb.sb("s3c", 128, 8)
    midc = lb.sb("midc", 128, 8)
    rdc = lb.sb("rdc", 128, 8)
    gs = lb.sb("gs", 128, 24)

    ps = [lb.ps(f"ps{i}") for i in range(8)]
    ld = lb.sem("ld")
    st = lb.sem("st")

    with lb.block() as blk:
        @blk.gpsimd
        def _(g):
            ld.inc(g.dma_start(out=gwm_b[:, :], in_=gwm_in), 16)
            ld.inc(g.dma_start(out=b2_s[:, :], in_=b2_in), 16)

        @blk.sync
        def _(sy):
            for dst, srcin in ((rows[0:1, 0:3 * T], ln3_in), (mod_s[:, :], mod_in),
                               (h30a[:, :], h30_in), (s0_s[:, :], s0_in),
                               (gwm_f[:, :], gwm_in), (gwl_s[:, :], gwl_in),
                               (gb_s[:, :], gb_in), (b1_s[:, :], b1_in),
                               (onesrf_s[:, :], onesrf_in), (cst_s[:, :], cst_in)):
                ld.inc(sy.dma_start(out=dst, in_=srcin), 16)

        @blk.scalar
        def _(s):
            ld.inc(s.dma_start(out=onescb_s[:, :], in_=onescb_in), 16)
            ld.inc(s.dma_start(out=onesrb_s[:, :], in_=onesrb_in), 16)
        ld_all = ld.n

        @blk.vector
        def _(v):
            v.wait_ge(ld.h, ld_all)

    e2 = lb.sem("e2")
    with lb.block() as blk:
        @blk.vector
        def _(v):
            v.tensor_add(h30v[:, :], h30a[:, 0:32], h30a[:, 32:64])
            v.tensor_add(h30v[:, :], h30v[:, :], h30a[:, 64:96])
            v.tensor_add(h30v[:, :], h30v[:, :], h30a[:, 96:128])
            v.tensor_reduce(lrow[0:1, 16:17], s0_s[0:1, 0:4], axis=AXX, op=ALU.add)
            v.tensor_reduce(lrow[0:1, 17:18], mod_s[0:1, 0:8:2], axis=AXX, op=ALU.add)
            v.tensor_reduce(lrow[0:1, 18:19], mod_s[0:1, 1:8:2], axis=AXX, op=ALU.add)
    with lb.block() as blk:
        @blk.tensor
        def _(t):
            e2.inc(t.matmul(ps[0][0:128, 0:1], onesrf_s[:, :], lrow[0:1, 16:17],
                            start=True, stop=True), 1)

        @blk.scalar
        def _(s):
            s.wait_ge(e2.h, 1)
            s.activation(s0col[:, :], ps[0][0:128, 0:1], AF.Copy)
            s.activation(lrow[0:1, 19:20], lrow[0:1, 17:18], AF.Copy,
                         scale=float(1.0 / (NV * D)))
            s.activation(lrow[0:1, 20:21], lrow[0:1, 18:19], AF.Copy,
                         scale=float(1.0 / (NTX * D)))
    with lb.block() as blk:
        @blk.vector
        def _(v):
            v.tensor_scalar(h30v[:, :], h30v[:, :], s0col[:, 0:1], None,
                            op0=ALU.subtract)
            v.tensor_scalar_add(lrow[0:1, 19:20], lrow[0:1, 19:20], 1e-8)
    with lb.block() as blk:
        @blk.vector
        def _(v):
            v.reciprocal(lrow[0:1, 21:22], lrow[0:1, 19:20])
    with lb.block() as blk:
        @blk.vector
        def _(v):
            v.tensor_tensor(lrow[0:1, 22:23], lrow[0:1, 20:21], lrow[0:1, 21:22],
                            op=ALU.mult)
    with lb.block() as blk:
        @blk.tensor
        def _(t):
            inst = None
            for fc in range(FT):
                inst = t.matmul(ps[1][0:1, 0:3], h30v[:, fc:fc + 1],
                                gwm_f[:, fc * 3:(fc + 1) * 3],
                                start=(fc == 0), stop=(fc == FT - 1))
            e2.inc(inst, 1)

        @blk.vector
        def _(v):
            v.tensor_scalar(rowb[0:1, 0:3], gwl_s[0:1, 0:3], lrow[0:1, 22:23],
                            None, op0=ALU.mult)
    with lb.block() as blk:
        @blk.vector
        def _(v):
            v.tensor_add(rowb[0:1, 0:3], rowb[0:1, 0:3], gb_s[0:1, 0:3])
    with lb.block() as blk:
        @blk.vector
        def _(v):
            v.wait_ge(e2.h, 2)
            v.scalar_tensor_tensor(lrow[0:1, 0:3], ps[1][0:1, 0:3], 1.0,
                                   rowb[0:1, 0:3], op0=ALU.mult, op1=ALU.add)
    with lb.block() as blk:
        @blk.vector
        def _(v):
            v.tensor_reduce(lrow[0:1, 4:5], lrow[0:1, 0:3], axis=AXX, op=ALU.max)
    with lb.block() as blk:
        @blk.vector
        def _(v):
            v.tensor_scalar(lrow[0:1, 8:11], lrow[0:1, 0:3], lrow[0:1, 4:5], None,
                            op0=ALU.is_ge)
    with lb.block() as blk:
        @blk.vector
        def _(v):
            v.scalar_tensor_tensor(lrow[0:1, 12:15], lrow[0:1, 8:11], -1e9,
                                   lrow[0:1, 0:3], op0=ALU.mult, op1=ALU.add)
    with lb.block() as blk:
        @blk.vector
        def _(v):
            v.tensor_reduce(lrow[0:1, 5:6], lrow[0:1, 12:15], axis=AXX, op=ALU.max)
    with lb.block() as blk:
        @blk.vector
        def _(v):
            v.tensor_scalar(lrow[0:1, 24:27], lrow[0:1, 12:15], lrow[0:1, 5:6],
                            None, op0=ALU.is_ge)
    e2b = lb.sem("e2b")
    with lb.block() as blk:
        @blk.tensor
        def _(t):
            for k in range(3):
                e2b.inc(t.matmul(ps[k][0:128, 0:1], onesrf_s[:, :],
                                 lrow[0:1, 8 + k:9 + k], start=True, stop=True), 1)
            for k in range(3):
                e2b.inc(t.matmul(ps[3 + k][0:128, 0:1], onesrf_s[:, :],
                                 lrow[0:1, 24 + k:25 + k], start=True, stop=True), 1)
            e2b.inc(t.matmul(ps[6][0:128, 0:3], onesrf_s[:, :], rowb[0:1, 0:3],
                             start=True, stop=True), 1)

        @blk.scalar
        def _(s):
            s.wait_ge(e2b.h, 7)
            for k in range(6):
                s.activation(acol[:, k:k + 1], ps[k][0:128, 0:1], AF.Copy)
            s.activation(rowbc[:, 0:3], ps[6][0:128, 0:3], AF.Copy)

    tvbc = lb.sbx("tvbc", 128, 2 * T, BF16)
    xs = [lb.sbx(f"xs{j}", 128, T, BF16) for j in range(2)]
    h3x = [lb.sbx(f"h3x{i}", 128, T, BF16) for i in range(FT)]
    emit_bc(lb, rows, rows_b, bc, onesrb_s, ps, lb.sem("cmBe2"))
    e25l = [lb.sem("e25l0"), lb.sem("e25l1")]
    e25f = lb.sem("e25f")
    with lb.block() as blk:
        @blk.gpsimd
        def _(g):
            for i in range(FT):
                if i >= 2:
                    g.wait_ge(e25f.h, i - 1)
                e25l[i % 2].inc(g.dma_start(out=xs[i % 2][:, :],
                                            in_=x3_in[i * 128:(i + 1) * 128, :]), 16)

        @blk.vector
        def _(v):
            for i in range(FT):
                v.wait_ge(e25l[i % 2].h, 16 * (i // 2 + 1))
                for tc in range(NTC):
                    sl = slice(tc * TC, (tc + 1) * TC)
                    v.tensor_tensor(h3x[i][:, sl], xs[i % 2][:, sl],
                                    bc[:, tc * TC:(tc + 1) * TC], op=ALU.mult)
                inst = None
                for tc in range(NTC):
                    sl = slice(tc * TC, (tc + 1) * TC)
                    inst = v.tensor_tensor(h3x[i][:, sl], h3x[i][:, sl],
                                           bc[:, T + tc * TC:T + (tc + 1) * TC],
                                           op=ALU.subtract)
                e25f.inc(inst, 1)

    gmm = lb.sem("gmm")
    gev = lb.sem("gev")
    with lb.block() as blk:
        @blk.tensor
        def _(t):
            for tt in range(8):
                if tt >= 2:
                    t.wait_ge(gev.h, tt - 1)
                inst = None
                for fc in range(FT):
                    inst = t.matmul(ps[tt % 2][0:128, 0:3],
                                    h3x[fc][:, tt * 128:(tt + 1) * 128],
                                    gwm_b[:, fc * 3:(fc + 1) * 3],
                                    start=(fc == 0), stop=(fc == FT - 1))
                gmm.inc(inst, 1)

        @blk.vector
        def _(v):
            for tt in range(8):
                v.wait_ge(gmm.h, tt + 1)
                gev.inc(v.scalar_tensor_tensor(gl_sb[:, tt * 3:(tt + 1) * 3],
                                               ps[tt % 2][0:128, 0:3], 1.0,
                                               rowbc[:, 0:3], op0=ALU.mult,
                                               op1=ALU.add), 1)
    with lb.block() as blk:
        @blk.vector
        def _(v):
            for tt in range(8):
                v.tensor_reduce(negm[:, tt:tt + 1], gl_sb[:, tt * 3:(tt + 1) * 3],
                                axis=AXX, op=ALU.max, negate=True)
    with lb.block() as blk:
        @blk.scalar
        def _(s):
            for tt in range(8):
                s.activation(gs[:, tt * 3:(tt + 1) * 3],
                             gl_sb[:, tt * 3:(tt + 1) * 3], AF.Exp,
                             bias=negm[:, tt:tt + 1],
                             accum_out=sumc[:, tt:tt + 1])
    with lb.block() as blk:
        @blk.vector
        def _(v):
            for tt in range(8):
                v.reciprocal(rsc[:, tt:tt + 1], sumc[:, tt:tt + 1])
    with lb.block() as blk:
        @blk.vector
        def _(v):
            for tt in range(8):
                v.tensor_scalar(gs[:, tt * 3:(tt + 1) * 3],
                                gs[:, tt * 3:(tt + 1) * 3], rsc[:, tt:tt + 1],
                                None, op0=ALU.mult)
    with lb.block() as blk:
        @blk.vector
        def _(v):
            for tt in range(8):
                v.tensor_reduce(tv0c[:, tt:tt + 1], gs[:, tt * 3:(tt + 1) * 3],
                                axis=AXX, op=ALU.max)
                v.tensor_reduce(mnc[:, tt:tt + 1], gs[:, tt * 3:(tt + 1) * 3],
                                axis=AXX, op=ALU.min)
                v.tensor_reduce(s3c[:, tt:tt + 1], gs[:, tt * 3:(tt + 1) * 3],
                                axis=AXX, op=ALU.add)
    with lb.block() as blk:
        @blk.vector
        def _(v):
            v.tensor_sub(midc[:, 0:8], s3c[:, 0:8], tv0c[:, 0:8])
    with lb.block() as blk:
        @blk.vector
        def _(v):
            v.tensor_sub(midc[:, 0:8], midc[:, 0:8], mnc[:, 0:8])
    with lb.block() as blk:
        @blk.vector
        def _(v):
            v.tensor_add(rdc[:, 0:8], tv0c[:, 0:8], midc[:, 0:8])
    with lb.block() as blk:
        @blk.vector
        def _(v):
            v.tensor_scalar_add(rdc[:, 0:8], rdc[:, 0:8], 1e-8)
    with lb.block() as blk:
        @blk.vector
        def _(v):
            v.reciprocal(rdc[:, 0:8], rdc[:, 0:8])
    with lb.block() as blk:
        @blk.vector
        def _(v):
            v.tensor_mul(tv0c[:, 0:8], tv0c[:, 0:8], rdc[:, 0:8])
            v.tensor_mul(midc[:, 0:8], midc[:, 0:8], rdc[:, 0:8])
    with lb.block() as blk:
        @blk.sync
        def _(sy):
            for tt in range(8):
                st.inc(sy.dma_start(out=tv_d[0:1, tt * 128:(tt + 1) * 128],
                                    in_=tv0c[:, tt:tt + 1]), 16)
                st.inc(sy.dma_start(out=tv_d[1:2, tt * 128:(tt + 1) * 128],
                                    in_=midc[:, tt:tt + 1]), 16)
            sy.wait_ge(st.h, st.n)
    with lb.block() as blk:
        @blk.vector
        def _(v):
            pass
    with lb.block() as blk:
        @blk.sync
        def _(sy):
            st.inc(sy.dma_start(out=R_(scr4, 0), in_=tv_d[0:1, :]), 16)
            st.inc(sy.dma_start(out=R_(scr4, 1), in_=tv_d[1:2, :]), 16)
            sy.wait_ge(st.h, st.n)
    with lb.block() as blk:
        @blk.vector
        def _(v):
            v.tensor_copy(R_(rows_b, 0), R_(scr4, 0))
            v.tensor_copy(R_(rows_b, 1), R_(scr4, 1))
    tvm = lb.sem("tvm")
    with lb.block() as blk:
        @blk.tensor
        def _(t):
            for r in range(2):
                for tc in range(NTC):
                    i = r * NTC + tc
                    tvm.inc(t.matmul(ps[i][0:128, 0:TC], onesrb_s[:, :],
                                     R_(rows_b, r)[:, tc * TC:(tc + 1) * TC],
                                     start=True, stop=True), 1)

        @blk.scalar
        def _(s):
            for r in range(2):
                for tc in range(NTC):
                    i = r * NTC + tc
                    s.wait_ge(tvm.h, i + 1)
                    s.activation(tvbc[:, r * T + tc * TC:r * T + (tc + 1) * TC],
                                 ps[i][0:128, 0:TC], AF.Copy)
    with lb.block() as blk:
        @blk.vector
        def _(v):
            for e in range(3):
                v.tensor_scalar(cbc[:, e * T:(e + 1) * T], tvbc[:, 0:T],
                                acol[:, e:e + 1], None, op0=ALU.mult)
    with lb.block() as blk:
        @blk.vector
        def _(v):
            for e in range(3):
                v.tensor_scalar(t1[:, :] if False else xs[0][:, 0:T],
                                tvbc[:, T:2 * T], acol[:, 3 + e:4 + e], None,
                                op0=ALU.mult)
                v.tensor_tensor(cbc[:, e * T:(e + 1) * T],
                                cbc[:, e * T:(e + 1) * T], xs[0][:, 0:T],
                                op=ALU.add)
    with lb.block() as blk:
        @blk.sync
        def _(sy):
            for e in range(3):
                st.inc(sy.dma_start(out=ceT[e:e + 1, :], in_=cbc[0:1, e * T:(e + 1) * T]), 16)
            sy.wait_ge(st.h, st.n)

    for i in reversed(range(FT)):
        lb.freex(f"h3x{i}")
    lb.freex("xs1")
    lb.freex("xs0")
    lb.freex("tvbc")

    act = [lb.sbx(f"act{k}", 128, TC, BF16) for k in range(96)]

    wls = [lb.sem("wls0"), lb.sem("wls1")]
    w2l = [lb.sem("w2l0"), lb.sem("w2l1")]
    emm = lb.sem("emm")
    eev = lb.sem("eev")
    edv = lb.sem("edv")
    omm = lb.sem("omm")
    oev = lb.sem("oev")
    osq = lb.sem("osq")
    osm = lb.sem("osm")
    ost = lb.sem("ost")
    hl = [lb.sem("hl0"), lb.sem("hl1")]
    hf = lb.sem("hf")
    for tcp in range(NTC):
        hl_b = [hl[0].n, hl[1].n]
        hf_b = hf.n
        with lb.block() as blk:
            @blk.gpsimd
            def _(g):
                for i in range(FT):
                    if i >= 2:
                        g.wait_ge(hf.h, hf_b + i - 1)
                    hl[i % 2].inc(g.dma_start(
                        out=xstc[i % 2][:, :],
                        in_=x3_in[i * 128:(i + 1) * 128, tcp * TC:(tcp + 1) * TC]), 16)

            @blk.vector
            def _(v):
                for i in range(FT):
                    v.wait_ge(hl[i % 2].h, hl_b[i % 2] + 16 * (i // 2 + 1))
                    v.tensor_tensor(h3tc[i][:, :], xstc[i % 2][:, :],
                                    bc[:, tcp * TC:(tcp + 1) * TC], op=ALU.mult)
                    hf.inc(v.tensor_tensor(h3tc[i][:, :], h3tc[i][:, :],
                                           bc[:, T + tcp * TC:T + (tcp + 1) * TC],
                                           op=ALU.subtract), 1)

        emm_b = emm.n
        eev_b = eev.n
        edv_b = edv.n
        with lb.block() as blk:
            @blk.gpsimd
            def _(g):
                for j in range(96):
                    if j >= 2:
                        g.wait_ge(emm.h, emm_b + j - 1)
                    wls[j % 2].inc(g.dma_start(
                        out=w1b[j % 2][:, :],
                        in_=w1_in[j * 128:(j + 1) * 128, :]), 16)

            @blk.tensor
            def _(t):
                for j in range(96):
                    t.wait_ge(wls[j % 2].h, 16 * (tcp * 48 + j // 2 + 1))
                    if j >= 2:
                        t.wait_ge(eev.h, eev_b + j - 1)
                    inst = None
                    for dt_ in range(FT):
                        inst = t.matmul(ps[j % 2][0:128, 0:TC],
                                        w1b[j % 2][:, dt_ * 128:(dt_ + 1) * 128],
                                        h3tc[dt_][:, :],
                                        start=(dt_ == 0), stop=(dt_ == FT - 1))
                    emm.inc(inst, 1)

            @blk.scalar
            def _(s):
                for j in range(96):
                    e = j // FT
                    s.wait_ge(emm.h, emm_b + j + 1)
                    if j >= 2:
                        s.wait_ge(edv.h, edv_b + j - 1)
                    eev.inc(s.activation(act[j][:, :], ps[j % 2][0:128, 0:TC],
                                         EFUNC[e], bias=b1_s[:, j:j + 1]), 1)

            @blk.vector
            def _(v):
                for j in range(96):
                    e = j // FT
                    v.wait_ge(eev.h, eev_b + j + 1)
                    edv.inc(v.tensor_tensor(act[j][:, :], act[j][:, :],
                                            cbc[:, e * T + tcp * TC:e * T + (tcp + 1) * TC],
                                            op=ALU.mult), 1)

        omm_b = omm.n
        oev_b = oev.n
        osq_b = osq.n
        osm_b = osm.n
        ost_b = ost.n
        with lb.block() as blk:
            @blk.gpsimd
            def _(g):
                for oc in range(FT):
                    for e in range(3):
                        k = oc * 3 + e
                        if k >= 2:
                            pk = k - 2
                            g.wait_ge(omm.h, omm_b + (pk // 3) * 4 + (pk % 3) + 1)
                        w2l[k % 2].inc(g.dma_start(
                            out=w2b[k % 2][:, :],
                            in_=w2_in[(e * FT + oc) * 128:(e * FT + oc + 1) * 128, :]), 16)

            @blk.tensor
            def _(t):
                for oc in range(FT):
                    if oc >= 2:
                        t.wait_ge(oev.h, oev_b + oc - 1)
                    inst = None
                    for e in range(3):
                        k = oc * 3 + e
                        t.wait_ge(w2l[k % 2].h, 16 * (tcp * 48 + k // 2 + 1))
                        for fc in range(FT):
                            inst = t.matmul(ps[oc % 2][0:128, 0:TC],
                                            w2b[k % 2][:, fc * 128:(fc + 1) * 128],
                                            act[e * FT + fc][:, :],
                                            start=(e == 0 and fc == 0), stop=False)
                        omm.inc(inst, 1)
                    omm.inc(t.matmul(ps[oc % 2][0:128, 0:TC],
                                     b2_s[0:3, oc * 128:(oc + 1) * 128],
                                     ceT[:, tcp * TC:(tcp + 1) * TC],
                                     start=False, stop=True), 1)
                    if oc >= 1:
                        po = oc - 1
                        t.wait_ge(oev.h, oev_b + po + 1)
                        t.matmul(ps[2 + tcp][0:1, 0:TC], onescb_s[:, :],
                                 moest[po % 2][:, :],
                                 start=(po == 0), stop=False)
                        t.wait_ge(osq.h, osq_b + po + 1)
                        osm.inc(t.matmul(ps[4 + tcp][0:1, 0:TC], onescb_s[:, :],
                                         sq[po % 2][:, :],
                                         start=(po == 0), stop=False), 1)
                po = FT - 1
                t.wait_ge(oev.h, oev_b + po + 1)
                t.matmul(ps[2 + tcp][0:1, 0:TC], onescb_s[:, :],
                         moest[po % 2][:, :], start=False, stop=True)
                t.wait_ge(osq.h, osq_b + po + 1)
                osm.inc(t.matmul(ps[4 + tcp][0:1, 0:TC], onescb_s[:, :],
                                 sq[po % 2][:, :], start=False, stop=True), 1)

            @blk.vector
            def _(v):
                for oc in range(FT):
                    v.wait_ge(omm.h, omm_b + (oc + 1) * 4)
                    if oc >= 2:
                        v.wait_ge(osm.h, osm_b + oc - 1)
                        v.wait_ge(ost.h, ost_b + 16 * (oc - 1))
                    oev.inc(v.tensor_copy(moest[oc % 2][:, :],
                                          ps[oc % 2][0:128, 0:TC]), 1)

            @blk.scalar
            def _(s):
                for oc in range(FT):
                    s.wait_ge(oev.h, oev_b + oc + 1)
                    if oc >= 2:
                        s.wait_ge(osm.h, osm_b + oc - 1)
                    osq.inc(s.activation(sq[oc % 2][:, :], moest[oc % 2][:, :],
                                         AF.Square), 1)

            @blk.sync
            def _(sy):
                for oc in range(FT):
                    sy.wait_ge(oev.h, oev_b + oc + 1)
                    ost.inc(sy.dma_start(
                        out=moe_d[oc * 128:(oc + 1) * 128, tcp * TC:(tcp + 1) * TC],
                        in_=moest[oc % 2][:, :]), 16)

    for k in reversed(range(96)):
        lb.freex(f"act{k}")
    xs6 = [lb.sbx(f"xs6_{j}", 128, T, BF16) for j in range(2)]
    t1 = lb.sbx("t1", 128, TC, BF16)
    t2 = lb.sbx("t2", 128, TC, BF16)
    xstf = [lb.sbx(f"xstf{j}", 128, TC) for j in range(2)]
    outt = [lb.sbx(f"outt{j}", 128, TC) for j in range(2)]

    emit_rows(lb, [ps[2][0:1, 0:TC], ps[3][0:1, 0:TC], ps[4][0:1, 0:TC], ps[5][0:1, 0:TC]], rows, scr4, cst_s)
    emit_bc(lb, rows, rows_b, bc, onesrb_s, ps, lb.sem("cmBe5"))

    xl6 = [lb.sem("xl60"), lb.sem("xl61")]
    ml6 = [lb.sem("ml60"), lb.sem("ml61")]
    m6f2 = lb.sem("m6f2")
    xf6 = lb.sem("xf6")
    st6 = lb.sem("st6")
    with lb.block() as blk:
        @blk.sync
        def _(sy):
            for oc in range(FT):
                if oc >= 2:
                    sy.wait_ge(m6f2.h, oc - 1)
                ml6[oc % 2].inc(sy.dma_start(out=xs6[oc % 2][:, :],
                                             in_=moe_d[oc * 128:(oc + 1) * 128, :]), 16)
                for tc in range(NTC):
                    i = oc * NTC + tc
                    if i >= 2:
                        sy.wait_ge(xf6.h, i - 1)
                    xl6[i % 2].inc(sy.dma_start(
                        out=xstf[i % 2][:, :],
                        in_=x3_in[oc * 128:(oc + 1) * 128, tc * TC:(tc + 1) * TC]), 16)

        @blk.vector
        def _(v):
            for oc in range(FT):
                v.wait_ge(ml6[oc % 2].h, 16 * (oc // 2 + 1))
                for tc in range(NTC):
                    i = oc * NTC + tc
                    sl = slice(tc * TC, (tc + 1) * TC)
                    v.wait_ge(xl6[i % 2].h, 16 * (i // 2 + 1))
                    if i >= 2:
                        v.wait_ge(st6.h, 16 * (i - 1))
                    tm = v.tensor_tensor(t1[:, :], xs6[oc % 2][:, sl], bc[:, sl],
                                         op=ALU.mult)
                    if tc == NTC - 1:
                        m6f2.inc(tm, 1)
                    v.tensor_tensor(t2[:, :], t1[:, :],
                                    bc[:, T + tc * TC:T + (tc + 1) * TC],
                                    op=ALU.subtract)
                    xf6.inc(v.tensor_tensor(outt[i % 2][:, :], xstf[i % 2][:, :],
                                            t2[:, :], op=ALU.add), 1)

        @blk.gpsimd
        def _(g):
            for oc in range(FT):
                for tc in range(NTC):
                    i = oc * NTC + tc
                    g.wait_ge(xf6.h, i + 1)
                    st6.inc(g.dma_start(
                        out=o_out[oc * 128:(oc + 1) * 128, tc * TC:(tc + 1) * TC],
                        in_=outt[i % 2][:, :]), 16)

    with lb.block() as blk:
        @blk.sync
        def _(sy):
            sy.wait_ge(st6.h, st6.n)
            sy.wait_ge(ost.h, ost.n)
            sy.wait_ge(st.h, st.n)

    lb.done()
    return nc



def _prep_A(inputs, core):
    b, cslot = divmod(core, 4)
    s0 = cslot * T
    x = np.asarray(inputs["L0_attn_x"])
    return {
        "x": _c(x[b, s0:s0 + T, :].T),
        "wq": _sbnative(np.asarray(inputs["L0_attn_Wq_d"])),
        "wqk": _c(np.asarray(inputs["L0_attn_W_qk"])),
        "wkv": _sbnative(np.asarray(inputs["L0_attn_Wkv_d"])),
        "bq": _c(np.asarray(inputs["L0_attn_bq_d"]).reshape(12, 1)),
        "bqkt": _c(np.asarray(inputs["L0_attn_b_qk"]).reshape(32, 4).T),
        "bkvbc": _c(np.broadcast_to(np.asarray(inputs["L0_attn_bkv_d"]), (128, 4))),
        "onescb": ONES_CB,
        "onesrb": ONES_RB,
        "cst": CST,
    }


def _prep_B(inputs, core, resA):
    b, cslot = divmod(core, 4)
    s0 = cslot * T
    x = np.asarray(inputs["L0_attn_x"])
    grp = range(4 * b, 4 * b + 4)
    ckv_full = np.concatenate([resA[c]["ckv"] for c in grp], axis=0)
    aug = np.concatenate([ckv_full, np.ones((4 * T, 1), np.float32)], axis=1)
    P = np.zeros((32, 128), np.float32)
    for h in range(32):
        P[h, 4 * h:4 * h + 4] = 1.0
    qm = np.zeros((1, T), np.float32)
    gl = np.arange(s0, s0 + T)
    qm[0, (gl >= 611) & (gl <= 4094)] = 1.0
    return {
        "x": _c(x[b, s0:s0 + T, :].T),
        "qp": _c(resA[core]["qp"]),
        "ckvf": _c(ckv_full.T),
        "ckva": _sbnative(aug),
        "wvut": _sbnative(_c(np.asarray(inputs["L0_attn_Wv_u"]).T)),
        "wob": _c(np.asarray(inputs["L0_attn_Wo"])),
        "w2f": _c(np.concatenate([_blocked(np.asarray(inputs["L1_ffn_w2"])[e])
                                  for e in range(3)], axis=0)),
        "boc": _cols(inputs["L0_attn_bo"]),
        "bvc": _cols(inputs["L0_attn_bv_u"]),
        "p32": P,
        "qmask": qm,
        "onescb": ONES_CB,
        "onesrb": ONES_RB,
        "cst": CST,
    }


def _prep_C(inputs, core, resB):
    b = core // 4
    grp = range(4 * b, 4 * b + 4)
    E8 = np.zeros((8, 128), np.float32)
    for e in range(8):
        E8[e, 16 * e:16 * (e + 1)] = 1.0
    W = np.asarray(inputs["L0_ffn_W"])
    A = np.asarray(inputs["L0_ffn_A"])
    Bm = np.asarray(inputs["L0_ffn_B"])
    qkvw = np.asarray(inputs["L1_attn_attn_qkvw"])
    qkvb = np.asarray(inputs["L1_attn_attn_qkvb"])
    vw = qkvw[:, 2 * D:]
    return {
        "x1": _c(resB[core]["x1"]),
        "r1": _c(resB[core]["r1"]),
        "qpall": _c(np.concatenate([resB[c]["qpart"] for c in grp], axis=1)),
        "s2all": _c(np.concatenate([resB[c]["s2"] for c in grp], axis=1)),
        "wtb": _blocked(_c(W.T)),
        "bcol": _cols(inputs["L0_ffn_b"]),
        "atsb": _sbnative(_c(A.transpose(2, 0, 1).reshape(D, 128))),
        "btn": _c(Bm.transpose(0, 2, 1).reshape(128, D)),
        "rwt": _sbnative(_c(np.asarray(inputs["L0_ffn_router_W"]).T)),
        "rb8": _c(np.asarray(inputs["L0_ffn_router_b"]).reshape(1, 8)),
        "e8": E8,
        "qkb": _blocked(qkvw[:, :2 * D]),
        "vwb": _c(vw.reshape(32, 128, 16, 256).transpose(2, 1, 0, 3).reshape(2048, 8192)),
        "qkbc": _cols(qkvb[:2 * D]),
        "vbbc": _c(np.broadcast_to(qkvb[2 * D:], (128, D))),
        "onescb": ONES_CB,
        "onesrb": ONES_RB,
        "onesrf": ONES_RF,
        "cst": CST,
    }


def _prep_D(inputs, core, resC):
    b, cslot = divmod(core, 4)
    s0 = cslot * T
    grp = range(4 * b, 4 * b + 4)
    K_full = np.concatenate([resC[c]["kf"] for c in grp], axis=1)
    V_full = np.concatenate([resC[c]["vt"] for c in grp], axis=0)
    gl = np.arange(s0, s0 + T)
    vm = (gl < 1228).astype(np.float32).reshape(1, T)
    tmr = (gl >= 1228).astype(np.float32).reshape(1, T)
    m0 = (gl == 0).astype(np.float32).reshape(1, T)
    return {
        "x2": _c(resC[core]["x2"]),
        "qf": _c(resC[core]["qf"]),
        "kfull": _c(K_full),
        "vblk": _blocked(V_full),
        "owb": _blocked(np.asarray(inputs["L1_attn_attn_ow"])),
        "w1f": _c(np.concatenate([_blocked(np.asarray(inputs["L1_ffn_w1"])[e])
                                  for e in range(3)], axis=0)),
        "obc": _cols(inputs["L1_attn_attn_ob"]),
        "vmr": vm,
        "tmr": tmr,
        "m0r": m0,
        "onescb": ONES_CB,
        "onesrb": ONES_RB,
        "onesrf": ONES_RF,
        "onescf": ONES_CF,
        "cst": CST,
    }


def _prep_E(inputs, core, resD, resB):
    b = core // 4
    grp = range(4 * b, 4 * b + 4)
    gw = np.asarray(inputs["L1_ffn_gate_w"])
    b1 = np.asarray(inputs["L1_ffn_b1"])
    return {
        "x3": _c(resD[core]["x3"]),
        "ln3": _c(resD[core]["ln3"]),
        "modall": _c(np.concatenate([resD[c]["modv"] for c in grp], axis=1)),
        "h30all": _c(np.concatenate([resD[c]["h30"] for c in grp], axis=1)),
        "s0all": _c(np.concatenate([resD[c]["s0"] for c in grp], axis=1)),
        "gwm": _sbnative(gw[:D, :]),
        "gwl": _c(gw[D:D + 1, :]),
        "gb": _c(np.asarray(inputs["L1_ffn_gate_b"]).reshape(1, 3)),
        "w1b": _c(resD[core]["w1bf"]),
        "w2b": _c(resB[core]["w2bf"]),
        "b1c": _c(np.concatenate([_cols(b1[e]) for e in range(3)], axis=1)),
        "b2n": _c(np.asarray(inputs["L1_ffn_b2"])),
        "onescb": ONES_CB,
        "onesrb": ONES_RB,
        "onesrf": ONES_RF,
        "cst": CST,
    }


def kernel(**inputs):
    inputs = {k: np.asarray(v) for k, v in inputs.items()}
    cores = list(range(NCORES))
    resA = run_bass_kernel_spmd(build_launch_A(),
                                [_prep_A(inputs, c) for c in cores],
                                core_ids=cores).results
    resB = run_bass_kernel_spmd(build_launch_B(),
                                [_prep_B(inputs, c, resA) for c in cores],
                                core_ids=cores).results
    resC = run_bass_kernel_spmd(build_launch_C(),
                                [_prep_C(inputs, c, resB) for c in cores],
                                core_ids=cores).results
    resD = run_bass_kernel_spmd(build_launch_D(),
                                [_prep_D(inputs, c, resC) for c in cores],
                                core_ids=cores).results
    resE = run_bass_kernel_spmd(build_launch_E(),
                                [_prep_E(inputs, c, resD, resB) for c in cores],
                                core_ids=cores).results
    out = np.zeros((2, 4 * T, D), np.float32)
    for c in cores:
        b, cs = divmod(c, 4)
        out[b, cs * T:(cs + 1) * T, :] = resE[c]["out"].T
    return out
